# version 55
# baseline (speedup 1.0000x reference)
import numpy as np
import ml_dtypes
import concourse.bass as bass
import concourse.mybir as mybir
from concourse.bass_utils import run_bass_kernel_spmd

F32 = mybir.dt.float32
BF16 = mybir.dt.bfloat16
U8 = mybir.dt.uint8
AF = mybir.ActivationFunctionType
ALU = mybir.AluOpType
AX = mybir.AxisListType.X
MAGIC = 12582912.0
NCORES = 8
D = 1024
NH = 8
KC = 8
HC1 = 32
KB = 1024


class Buf:
    __slots__ = ("w", "r")

    def __init__(self, carry=()):
        self.w = None
        self.r = []
        for b in carry:
            if b.w is not None:
                self.r.append(b.w)
            self.r.extend(b.r)


class Prog:
    ENGS = ("tensor", "vector", "scalar", "gpsimd", "sync")
    NRING = 8

    def __init__(self, nc):
        self.nc = nc
        names = ["pe", "dve", "act", "pool", "cc"]
        names += [f"ds{i}" for i in range(self.NRING)]
        names += [f"dg{i}" for i in range(self.NRING)]
        self.sem = {n: nc.alloc_semaphore("s_" + n) for n in names}
        self.cnt = {n: 0 for n in names}
        self.items = {e: [] for e in self.ENGS}
        self.waited = {e: {n: 0 for n in names} for e in self.ENGS}
        self.rr = {"sync": 0, "gpsimd": 0}

    def emit(self, eng, fn, r=(), w=(), sem=None, extra=()):
        if sem is None:
            sem = {"tensor": ("pe", 1), "vector": ("dve", 1),
                   "scalar": ("act", 1), "gpsimd": ("pool", 1)}[eng]
        sname, amt = sem
        deps = {}

        def dep(tok):
            if tok is not None:
                s, v = tok
                if deps.get(s, 0) < v:
                    deps[s] = v
        for b in r:
            dep(b.w)
        for b in w:
            dep(b.w)
            for t in b.r:
                dep(t)
        for t in extra:
            dep(t)
        own = {"tensor": "pe", "vector": "dve", "scalar": "act",
               "gpsimd": "pool"}.get(eng)
        waits = []
        for s, v in deps.items():
            if s == own or self.waited[eng][s] < v:
                waits.append((s, v))
                if self.waited[eng][s] < v:
                    self.waited[eng][s] = v
        self.cnt[sname] += amt
        tok = (sname, self.cnt[sname])
        self.items[eng].append((waits, fn, self.sem[sname], amt))
        for b in r:
            b.r.append(tok)
        for b in w:
            b.w = tok
            b.r = []
        return tok

    def dma(self, queue, fn, r=(), w=(), extra=()):
        pfx = "ds" if queue == "sync" else "dg"
        slot = self.rr[queue] % self.NRING
        self.rr[queue] += 1
        sname = f"{pfx}{slot}"
        gate = (sname, self.cnt[sname]) if self.cnt[sname] else None
        extra = list(extra) + ([gate] if gate else [])
        return self.emit(queue, fn, r=r, w=w, sem=(sname, 16), extra=extra)

    def finish(self):
        finals = dict(self.cnt)
        with self.nc.Block() as block:
            for eng_name in self.ENGS:
                def make(engname):
                    def body(eng):
                        for waits, fn, semh, amt in self.items[engname]:
                            for s, v in waits:
                                eng.wait_ge(self.sem[s], v)
                            inst = fn(eng)
                            inst.then_inc(semh, amt)
                        for s, v in finals.items():
                            if v > self.waited[engname][s]:
                                eng.wait_ge(self.sem[s], v)
                    return body
                getattr(block, eng_name)(make(eng_name))


class Region:
    def __init__(self, handle, nbytes):
        self.h = handle
        self.nbytes = nbytes
        self.carves = []

    def carve(self, start, nbytes, dtype, ncols=None):
        assert start % 32 == 0 and start + nbytes <= self.nbytes, (start, nbytes)
        ap = self.h.ap()[:, start:start + nbytes]
        if dtype != U8:
            ap = ap.bitcast(dtype)
        if ncols is not None:
            ap = ap[:, 0:ncols]
        carry = [b for s, e, b in self.carves if s < start + nbytes and start < e]
        buf = Buf(carry)
        self.carves.append((start, start + nbytes, buf))
        return ap, buf

    def carve_n(self, start, n, step, dtype, ncols=None):
        aps, bufs = [], []
        for i in range(n):
            ap, b = self.carve(start + i * step, step, dtype, ncols)
            aps.append(ap)
            bufs.append(b)
        return aps, bufs


def build(T=2048):
    S = min(512, T)
    NSL = T // S
    TCS = S // 128
    NTC = T // 128
    THW = min(1024, T)
    NTH2 = T // THW
    TQW = 256
    assert T % S == 0 and S % 256 == 0

    nc = bass.Bass("TRN2", target_bir_lowering=False)

    xT = nc.dram_tensor("xT", [D, T], F32, kind="ExternalInput")
    wqkvT_d = nc.dram_tensor("wqkvT", [D, 3 * D], F32, kind="ExternalInput")
    woutT_d = nc.dram_tensor("woutT", [D, D], F32, kind="ExternalInput")
    projE_d = nc.dram_tensor("projE", [128, NH * 128], F32, kind="ExternalInput")
    boutpp_d = nc.dram_tensor("boutpp", [128, 8], F32, kind="ExternalInput")
    b1pp_d = nc.dram_tensor("b1pp", [128, 32], F32, kind="ExternalInput")
    b2row_d = nc.dram_tensor("b2row", [1, D], F32, kind="ExternalInput")
    w1blk_d = nc.dram_tensor("w1blk", [HC1, KC, 128, 128], F32, kind="ExternalInput")
    w2T_d = nc.dram_tensor("w2T", [4096, D], F32, kind="ExternalInput")
    idf_d = nc.dram_tensor("idf", [128, 128], F32, kind="ExternalInput")
    idb_d = nc.dram_tensor("idb", [128, 128], BF16, kind="ExternalInput")
    onesr_d = nc.dram_tensor("onesr", [1, 128], F32, kind="ExternalInput")
    onesb_d = nc.dram_tensor("onesb", [1, 128], BF16, kind="ExternalInput")
    invd_d = nc.dram_tensor("invd", [128, 1], BF16, kind="ExternalInput")
    cvec_d = nc.dram_tensor("cvec", [128, 4], F32, kind="ExternalInput")
    out_d = nc.dram_tensor("out", [T, D], F32, kind="ExternalOutput")

    hsp_d = nc.dram_tensor("hsp", [NTC, HC1, 128, 128], F32)
    hspb = [Buf() for _ in range(NTC)]
    mr_d = nc.dram_tensor("mrsp", [NSL, 2 * S], F32)
    mrb = [Buf() for _ in range(NSL)]
    cc1i_d = nc.dram_tensor("cc1i", [128, 1], F32)
    cc1o_d = nc.dram_tensor("cc1o", [128 * NCORES, 1], F32)
    cc2i_d = nc.dram_tensor("cc2i", [128, 1], F32)
    cc2o_d = nc.dram_tensor("cc2o", [128 * NCORES, 1], F32)
    cc1ib, cc1ob, cc2ib, cc2ob = Buf(), Buf(), Buf(), Buf()
    RG = [list(range(NCORES))]

    def sbt(name, shape, dt):
        return nc.alloc_sbuf_tensor(name, shape, dt)

    X2 = [sbt(f"x2_{k}", [128, T], F32) for k in range(KC)]
    X2B = [[Buf() for _ in range(NSL)] for _ in range(KC)]
    VECS = sbt("vecs", [128, 2 * S], F32)
    vmb, vrb = Buf(), Buf()
    SCAL = sbt("scal", [128, 64], F32)
    (SC_AMX, SC_TX, SC_SX, SC_SW1, SC_S1X, SC_AMH, SC_TH, SC_SH, SC_SW2,
     SC_S2X, SC_W1M, SC_W2M, SC_T1, SC_T2, SC_INVX, SC_INVW1, SC_INVH,
     SC_INVW2) = range(18)
    scb = [Buf() for _ in range(20)]
    BCV = sbt("bcv", [128, 16], F32)
    (BC_INVX, BC_S1X, BC_INVW1, BC_INVH, BC_S2X, BC_INVW2, BC_LMAX, BC_W1M,
     BC_W2M, BC_HMAX, BC_Z0, BC_Z1) = range(12)
    bcb = [Buf() for _ in range(16)]
    POOLT = [sbt(f"poolt{i}", [128, 1024], F32) for i in range(2)]
    ptb = [Buf(), Buf()]
    pt1h = [Buf(), Buf()]
    EXPT = [sbt(f"expt{i}", [128, 512], F32) for i in range(2)]
    exb = [Buf(), Buf()]
    STSB = [sbt(f"stsb{i}", [128, 64], BF16) for i in range(2)]
    stb = [Buf() for _ in range(2)]
    ATOK = [sbt(f"atok{i}", [128, 128], F32) for i in range(4)]
    atb = [Buf() for _ in range(4)]
    ZV = sbt("zv", [128, 8], F32)
    zvb = [Buf(), Buf()]
    IDF = sbt("idf_s", [128, 128], F32)
    IDB = sbt("idb_s", [128, 128], BF16)
    ONESR = sbt("onesr_s", [1, 128], F32)
    ONESB = sbt("onesb_s", [1, 128], BF16)
    INVD = sbt("invd_s", [128, 1], BF16)
    B1PP = sbt("b1pp_s", [128, 32], F32)
    B1S = sbt("b1s_s", [128, 32], F32)
    BOUT = sbt("bout_s", [128, 8], F32)
    B2R = sbt("b2r_s", [1, D], BF16)
    CV = sbt("cv_s", [128, 4], F32)
    cb = {n: Buf() for n in ("idf", "idb", "onesr", "onesb", "invd", "b1pp",
                             "b1s", "bout", "b2r", "proj", "cv")}

    AR_BYTES = 120 * KB
    ar_h = sbt("arena", [128, AR_BYTES], U8)
    AR = Region(ar_h, AR_BYTES)

    WQKV, wqkvb = AR.carve_n(0, KC, 6 * KB, BF16)
    WOUTS, woutb = AR.carve_n(48 * KB, KC, 2 * KB, BF16)
    LN1S, ln1b = AR.carve_n(64 * KB, KC, 1 * KB, BF16, ncols=S)
    QF, qfb = AR.carve_n(72 * KB, 2, 1 * KB, BF16, ncols=S)
    KF, kfb = AR.carve_n(74 * KB, 2, 1 * KB, BF16, ncols=S)
    QP, qpb = AR.carve_n(76 * KB, 2, 1 * KB, BF16, ncols=S)
    KP, kpb = AR.carve_n(78 * KB, 2, 1 * KB, BF16, ncols=S)
    AFM, afb = AR.carve_n(80 * KB, KC, 1 * KB, BF16, ncols=S)
    VT, vtb = AR.carve_n(88 * KB, TCS, 2080, BF16, ncols=1032)
    XB16, xbb = AR.carve_n(97 * KB, KC, 1 * KB, BF16, ncols=S)
    SCANB, scnb = AR.carve_n(105 * KB, 2, 2 * KB, F32)
    MBR, mbrb = AR.carve_n(109 * KB, 2, 2 * KB, F32, ncols=S)
    SQB, sqb = AR.carve_n(113 * KB, 4, 512, BF16, ncols=TQW)
    PROJS_l, projb_l = AR.carve_n(115 * KB, 1, 2 * KB, BF16)
    PROJS = PROJS_l[0]
    cb["proj"] = projb_l[0]

    PBW = [nc.alloc_psum_tensor(f"pbw{i}", [128, 1024], F32) for i in range(2)]
    pwb = [[Buf(), Buf()], [Buf(), Buf()]]
    PM = [nc.alloc_psum_tensor(f"pm{i}", [128, 512], F32) for i in range(2)]
    pmb = [Buf(), Buf()]
    CB = [nc.alloc_psum_tensor(f"cb{i}", [128, 512], F32) for i in range(2)]
    cbb = [Buf(), Buf()]
    CB_ST = [CB[i][:, 0:64] for i in range(2)]
    CB_NUM = [[CB[i][:, 64:193], CB[i][:, 200:329]] for i in range(2)]
    CB_DEN2 = [CB[i].rearrange("p (a b) -> p a b", b=1)[:, :, 0]
               for i in range(2)]
    CB_TR = [CB[i][:, 336:464] for i in range(2)]

    pr = Prog(nc)
    E = pr.emit
    DMA = pr.dma

    DMA("sync", lambda e: e.dma_start(out=IDF[:], in_=idf_d[:]), w=[cb["idf"]])
    DMA("sync", lambda e: e.dma_start(out=IDB[:], in_=idb_d[:]), w=[cb["idb"]])
    DMA("sync", lambda e: e.dma_start(out=ONESR[:], in_=onesr_d[:]), w=[cb["onesr"]])
    DMA("sync", lambda e: e.dma_start(out=ONESB[:], in_=onesb_d[:]), w=[cb["onesb"]])
    DMA("sync", lambda e: e.dma_start(out=INVD[:], in_=invd_d[:]), w=[cb["invd"]])
    DMA("sync", lambda e: e.dma_start(out=B1PP[:], in_=b1pp_d[:]), w=[cb["b1pp"]])
    DMA("sync", lambda e: e.dma_start(out=BOUT[:], in_=boutpp_d[:]), w=[cb["bout"]])
    DMA("sync", lambda e: e.dma_start(out=CV[:], in_=cvec_d[:]), w=[cb["cv"]])
    DMA("gpsimd", lambda e: e.dma_start(out=B2R[:], in_=b2row_d[:]), w=[cb["b2r"]])
    DMA("gpsimd", lambda e: e.dma_start(out=PROJS[:], in_=projE_d[:]), w=[cb["proj"]])
    for k in range(KC):
        DMA("sync", lambda e, k=k: e.dma_start(
            out=X2[k][:, 0:S], in_=xT[k * 128:(k + 1) * 128, 0:S]),
            w=[X2B[k][0]])
    vcast_copies = []
    if T >= 2048:
        for k in range(KC):
            DMA("sync", lambda e, k=k: e.dma_start(
                out=X2[k][:, 512:1536],
                in_=wqkvT_d[k * 128:(k + 1) * 128, 2048:3072]),
                w=[X2B[k][1], X2B[k][2]])

            def _cp(k=k):
                E("scalar", lambda e, k=k: e.copy(
                    WQKV[k][:, 2048:3072], X2[k][:, 512:1536]),
                  r=[X2B[k][1], X2B[k][2]], w=[wqkvb[k]])
            vcast_copies.append(_cp)
        parts = ((0, 1024), (1024, 2048))
    else:
        parts = ((2048, 3072), (0, 1024), (1024, 2048))
    for part0, part1 in parts:
        for k in range(KC):
            DMA("gpsimd", lambda e, k=k, part0=part0, part1=part1: e.dma_start(
                out=WQKV[k][:, part0:part1],
                in_=wqkvT_d[k * 128:(k + 1) * 128, part0:part1]),
                w=[wqkvb[k]])
    for k in range(KC):
        DMA("gpsimd", lambda e, k=k: e.dma_start(
            out=WOUTS[k][:], in_=woutT_d[k * 128:(k + 1) * 128, :]), w=[woutb[k]])
    for tc in range(TCS):
        E("vector", lambda e, tc=tc: e.memset(VT[tc][:], 1.0), w=[vtb[tc]])
    for c in (BC_W1M, BC_W2M, BC_LMAX, BC_HMAX):
        E("vector", lambda e, c=c: e.memset(BCV[:, c:c + 1], 0.0), w=[bcb[c]])

    w1_tiles = [(hc, hf) for hc in range(HC1) for hf in range(2)]
    sc_dma = [0]
    sc_red = [0]

    def scan_dmas(n):
        for _ in range(n):
            j = sc_dma[0]
            if j >= len(w1_tiles):
                return
            sc_dma[0] += 1
            hc, hf = w1_tiles[j]
            i = j % 2
            src_ap = w1blk_d[hc, hf * 4:(hf + 1) * 4].rearrange("kc p h -> p kc h")
            DMA("gpsimd", lambda e, src_ap=src_ap, i=i: e.dma_start(
                out=SCANB[i].rearrange("p (kc h) -> p kc h", h=128), in_=src_ap),
                w=[scnb[i]])

    def scan_reds(n):
        for _ in range(n):
            if sc_red[0] >= sc_dma[0] or sc_red[0] >= len(w1_tiles):
                return
            j = sc_red[0]
            sc_red[0] += 1
            i = j % 2
            E("vector", lambda e, i=i: e.tensor_reduce(
                BCV[:, BC_Z0:BC_Z0 + 1], SCANB[i][:], AX, ALU.max,
                apply_absolute_value=True),
              r=[scnb[i]], w=[bcb[BC_Z0]])
            E("vector", lambda e: e.tensor_tensor(
                BCV[:, BC_W1M:BC_W1M + 1], BCV[:, BC_W1M:BC_W1M + 1],
                BCV[:, BC_Z0:BC_Z0 + 1], ALU.max),
              r=[bcb[BC_Z0]], w=[bcb[BC_W1M]])

    def scan_step():
        if sc_red[0] + 2 <= sc_dma[0]:
            scan_reds(2)
        scan_dmas(2)

    def ln_stats(src_aps, src_bufs):
        for tq in range(S // TQW):
            E("tensor", lambda e, tq=tq: [
                e.matmul(PM[0][0:1, 0:TQW], INVD[:],
                         src_aps[k][:, tq * TQW:(tq + 1) * TQW],
                         start=(k == 0), stop=(k == KC - 1))
                for k in range(KC)][-1],
              r=[cb["invd"]] + list(src_bufs), w=[pmb[0]])
            for k in range(KC):
                i = k % 4
                E("vector", lambda e, k=k, i=i, tq=tq: e.tensor_tensor(
                    SQB[i][:], src_aps[k][:, tq * TQW:(tq + 1) * TQW],
                    src_aps[k][:, tq * TQW:(tq + 1) * TQW], ALU.mult),
                  r=[src_bufs[k]], w=[sqb[i]])
                if k % 4 == 3:
                    E("tensor", lambda e, k=k: [
                        e.matmul(PM[0][0:1, TQW:2 * TQW], INVD[:], SQB[j][:],
                                 start=(k - 3 + j == 0),
                                 stop=(k - 3 + j == KC - 1))
                        for j in range(4)][-1],
                      r=[cb["invd"]] + [sqb[j] for j in range(4)], w=[pmb[0]])
            c0 = tq * TQW
            E("vector", lambda e, c0=c0: e.tensor_copy(
                VECS[0:1, c0:c0 + TQW], PM[0][0:1, 0:TQW]),
              r=[pmb[0]], w=[vmb])
            E("vector", lambda e, c0=c0: e.tensor_tensor(
                POOLT[0][0:1, 0:TQW], VECS[0:1, c0:c0 + TQW],
                VECS[0:1, c0:c0 + TQW], ALU.mult),
              r=[vmb], w=[ptb[0]])
            E("vector", lambda e: e.tensor_tensor(
                POOLT[0][0:1, 256:256 + TQW], PM[0][0:1, TQW:2 * TQW],
                POOLT[0][0:1, 0:TQW], ALU.subtract),
              r=[pmb[0], ptb[0]], w=[ptb[0]])
            E("scalar", lambda e: e.activation(
                POOLT[0][0:1, 0:TQW], POOLT[0][0:1, 256:256 + TQW], AF.Ln,
                bias=CV[0:1, 1:2]),
              r=[ptb[0], cb["cv"]], w=[ptb[0]])
            E("scalar", lambda e, c0=c0: e.activation(
                VECS[0:1, S + c0:S + c0 + TQW], POOLT[0][0:1, 0:TQW], AF.Exp,
                scale=CV[0:1, 2:3]),
              r=[ptb[0], cb["cv"]], w=[vrb])

    def bcast_mr(eng="vector"):
        def cp(dst, srcap):
            if eng == "vector":
                E("vector", lambda e, dst=dst, srcap=srcap: e.tensor_copy(
                    dst[0][:], srcap), r=[pmb[1]], w=[dst[1]])
            else:
                E("scalar", lambda e, dst=dst, srcap=srcap: e.copy(
                    dst[0][:], srcap), r=[pmb[1]], w=[dst[1]])
        E("tensor", lambda e: e.matmul(
            PM[1][:, 0:S], ONESR[:], VECS[0:1, 0:S], start=True, stop=True),
          r=[cb["onesr"], vmb], w=[pmb[1]])
        cp((MBR[0], mbrb[0]), PM[1][:, 0:S])
        E("tensor", lambda e: e.matmul(
            PM[1][:, 0:S], ONESR[:], VECS[0:1, S:2 * S], start=True, stop=True),
          r=[cb["onesr"], vrb], w=[pmb[1]])
        cp((MBR[1], mbrb[1]), PM[1][:, 0:S])

    def bcast_scal(col, sbuf_, bc):
        E("tensor", lambda e, col=col: e.matmul(
            PM[1][:, 0:1], ONESR[:], SCAL[0:1, col:col + 1],
            start=True, stop=True),
          r=[cb["onesr"], sbuf_], w=[pmb[1]])
        E("vector", lambda e, bc=bc: e.tensor_copy(
            BCV[:, bc:bc + 1], PM[1][:, 0:1]),
          r=[pmb[1]], w=[bcb[bc]])

    slot_ctr = [0]

    def next_slot():
        n = slot_ctr[0] % 4
        slot_ctr[0] += 1
        i, j = n // 2, n % 2
        return PBW[i][:, j * 512:(j + 1) * 512], pwb[i][j]

    def x2_load(slc):
        t0 = slc * S
        for k in range(KC):
            DMA("sync", lambda e, k=k, t0=t0: e.dma_start(
                out=X2[k][:, t0:t0 + S],
                in_=xT[k * 128:(k + 1) * 128, t0:t0 + S]),
                w=[X2B[k][slc]])

    def stage_ln_stats(slc):
        t0 = slc * S
        for k in range(KC):
            E("scalar", lambda e, k=k, t0=t0: e.copy(
                XB16[k][:], X2[k][:, t0:t0 + S]),
              r=[X2B[k][slc]], w=[xbb[k]])
        ln_stats([XB16[k] for k in range(KC)], xbb)
        bcast_mr()

    def stage_ln_norm(slc):
        t0 = slc * S
        for k in range(KC):
            hh = k % 2
            E("gpsimd", lambda e, k=k, t0=t0, hh=hh: e.tensor_tensor(
                POOLT[1][:, hh * 512:hh * 512 + S], X2[k][:, t0:t0 + S],
                MBR[0][:], ALU.subtract),
              r=[X2B[k][slc], mbrb[0]], w=[pt1h[hh]])
            E("vector", lambda e, k=k, hh=hh: e.tensor_tensor(
                LN1S[k][:], POOLT[1][:, hh * 512:hh * 512 + S], MBR[1][:],
                ALU.mult),
              r=[pt1h[hh], mbrb[1]], w=[ln1b[k]])

    stage_ln_stats(0)
    stage_ln_norm(0)
    for cp in vcast_copies:
        cp()
    for slc in range(NSL):
        t0 = slc * S
        if slc + 1 < NSL:
            x2_load(slc + 1)
        for tc in range(TCS):
            for half in range(2):
                ps, psb = next_slot()
                if tc == 0:
                    for k in range(KC):
                        E("tensor", lambda e, tc=tc, half=half, ps=ps, k=k:
                            e.matmul(ps, LN1S[k][:, tc * 128:(tc + 1) * 128],
                                     WQKV[k][:, 2048 + half * 512:
                                              2048 + (half + 1) * 512],
                                     start=(k == 0), stop=(k == KC - 1)),
                          r=[ln1b[k], wqkvb[k]], w=[psb])
                else:
                    E("tensor", lambda e, tc=tc, half=half, ps=ps: [
                        e.matmul(ps, LN1S[k][:, tc * 128:(tc + 1) * 128],
                                 WQKV[k][:, 2048 + half * 512:
                                          2048 + (half + 1) * 512],
                                 start=(k == 0), stop=(k == KC - 1))
                        for k in range(KC)][-1],
                      r=[ln1b[kk] for kk in range(KC)]
                        + [wqkvb[kk] for kk in range(KC)],
                      w=[psb])
                E("vector", lambda e, tc=tc, half=half, ps=ps: e.tensor_copy(
                    VT[tc].rearrange("p (h c) -> p h c", c=129)
                        [:, 4 * half:4 * half + 4, 0:128],
                    ps.rearrange("p (h t) -> p h t", t=128)),
                  r=[psb], w=[vtb[tc]])

        def qk_group(wq_oc, h, dst_aps, dst_bufs):
            hb = h % 2
            ps, psb = next_slot()
            E("tensor", lambda e, wq_oc=wq_oc, ps=ps: [
                e.matmul(ps[:, 0:S], WQKV[k][:, wq_oc * 128:(wq_oc + 1) * 128],
                         LN1S[k][:], start=(k == 0), stop=(k == KC - 1))
                for k in range(KC)][-1],
              r=[ln1b[kk] for kk in range(KC)] + [wqkvb[kk] for kk in range(KC)],
              w=[psb])
            E("scalar", lambda e, hb=hb, ps=ps: e.copy(
                dst_aps[hb][:], ps[:, 0:S]),
              r=[psb], w=[dst_bufs[hb]])

        def proj_elu(h, dst_aps, dst_bufs, pdst_aps, pdst_bufs, eb):
            hb = h % 2
            ps2, psb2 = next_slot()
            E("tensor", lambda e, h=h, hb=hb, ps2=ps2: e.matmul(
                ps2[:, 0:S], PROJS[:, h * 128:(h + 1) * 128], dst_aps[hb][:],
                start=True, stop=True),
              r=[cb["proj"], dst_bufs[hb]], w=[psb2])
            E("scalar", lambda e, ps2=ps2, eb=eb: e.activation(
                EXPT[eb][:, 0:S], ps2[:, 0:S], AF.Exp),
              r=[psb2], w=[exb[eb]])
            E("vector", lambda e, eb=eb: e.tensor_scalar(
                EXPT[eb][:, 0:S], EXPT[eb][:, 0:S], 1.0, None, ALU.min),
              r=[exb[eb]], w=[exb[eb]])
            E("vector", lambda e, hb=hb, ps2=ps2, eb=eb: e.scalar_tensor_tensor(
                pdst_aps[hb][:], ps2[:, 0:S], 0.0, EXPT[eb][:, 0:S],
                ALU.max, ALU.add),
              r=[psb2, exb[eb]], w=[pdst_bufs[hb]])

        def chains_part1_st(h):
            hb = h % 2
            for b in range(TCS // 2):
                E("tensor", lambda e, h=h, b=b, hb=hb: [
                    e.matmul(CB_ST[b][32 * c:32 * (c + 1), 32 * t:32 * (t + 1)],
                             KP[hb][:, (2 * b + t) * 128 + 32 * c:
                                      (2 * b + t) * 128 + 32 * (c + 1)],
                             QP[hb][:, (2 * b + t) * 128 + 32 * c:
                                      (2 * b + t) * 128 + 32 * (c + 1)],
                             start=True, stop=True, tile_position=(0, 32 * c))
                    for t in range(2) for c in range(4)][-1],
                  r=[kpb[hb], qpb[hb]], w=[cbb[b]])
                E("scalar", lambda e, b=b: e.copy(STSB[b][:], CB_ST[b]),
                  r=[cbb[b]], w=[stb[b]])

        def chains_part1_num(h):
            for b in range(TCS // 2):
                E("tensor", lambda e, h=h, b=b: [
                    e.matmul(CB_NUM[b][t][32 * c:32 * (c + 1), :],
                             STSB[b][32 * c:32 * (c + 1), 32 * t:32 * (t + 1)],
                             VT[2 * b + t][32 * c:32 * (c + 1),
                                           h * 129:(h + 1) * 129],
                             start=True, stop=True, tile_position=(32 * c, 32 * c))
                    for t in range(2) for c in range(4)][-1],
                  r=[stb[b], vtb[2 * b], vtb[2 * b + 1]], w=[cbb[b]])

        def chains_part1(h):
            chains_part1_st(h)
            chains_part1_num(h)

        def chains_part2z(h):
            for b in range(TCS // 2):
                for t in range(2):
                    E("vector", lambda e, b=b, t=t: e.tensor_scalar(
                        ZV[:, 4 * b + t:4 * b + t + 1],
                        CB_NUM[b][t][:, 128:129], 1e-8, None, ALU.add),
                      r=[cbb[b]], w=[zvb[b]])
                E("vector", lambda e, b=b: e.reciprocal(
                    ZV[:, 4 * b + 2:4 * b + 4], ZV[:, 4 * b:4 * b + 2]),
                  r=[zvb[b]], w=[zvb[b]])
                for t in range(2):
                    E("vector", lambda e, b=b, t=t: e.tensor_scalar(
                        ATOK[2 * b + t][:], CB_NUM[b][t][:, 0:128],
                        ZV[:, 4 * b + 2 + t:4 * b + 3 + t], None, ALU.mult),
                      r=[cbb[b], zvb[b]], w=[atb[2 * b + t]])

        def chains_part2t(h):
            for t in range(2):
                for b in range(TCS // 2):
                    E("tensor", lambda e, b=b, t=t: e.transpose(
                        CB_TR[b], ATOK[2 * b + t][:], IDF[:]),
                      r=[atb[2 * b + t], cb["idf"]], w=[cbb[b]])
                    E("scalar", lambda e, h=h, b=b, t=t: e.copy(
                        AFM[h][:, (2 * b + t) * 128:(2 * b + t + 1) * 128],
                        CB_TR[b]),
                      r=[cbb[b]], w=[afb[h]])

        pend = None
        pend_t = None
        for h in range(NH):
            qk_group(h, h, QF, qfb)
            if pend_t is not None:
                chains_part2t(pend_t)
                pend_t = None
            if pend is not None:
                chains_part1_st(pend)
            qk_group(NH + h, h, KF, kfb)
            if pend is not None:
                chains_part1_num(pend)
                chains_part2z(pend)
            proj_elu(h, QF, qfb, QP, qpb, 0)
            proj_elu(h, KF, kfb, KP, kpb, 1)
            if pend is not None:
                pend_t = pend
            pend = h
            scan_step()
        if pend_t is not None:
            chains_part2t(pend_t)
        chains_part1(pend)
        chains_part2z(pend)
        chains_part2t(pend)

        for oc in range(KC):
            ps, psb = next_slot()
            E("tensor", lambda e, oc=oc, ps=ps: [
                e.matmul(ps[:, 0:S], WOUTS[k][:, oc * 128:(oc + 1) * 128],
                         AFM[k][:], start=(k == 0), stop=(k == KC - 1))
                for k in range(KC)][-1],
              r=[afb[kk] for kk in range(KC)] + [woutb[kk] for kk in range(KC)],
              w=[psb])
            E("vector", lambda e, oc=oc, ps=ps, t0=t0: e.scalar_tensor_tensor(
                X2[oc][:, t0:t0 + S], ps[:, 0:S], BOUT[:, oc:oc + 1],
                X2[oc][:, t0:t0 + S], ALU.add, ALU.add),
              r=[psb, cb["bout"], X2B[oc][slc]], w=[X2B[oc][slc]])

        if slc + 1 < NSL:
            stage_ln_stats(slc + 1)
            stage_ln_norm(slc + 1)

        for k in range(KC):
            E("scalar", lambda e, k=k, t0=t0: e.copy(
                XB16[k][:], X2[k][:, t0:t0 + S]),
              r=[X2B[k][slc]], w=[xbb[k]])
        ln_stats([XB16[k] for k in range(KC)], xbb)
        DMA("sync", lambda e, slc=slc: e.dma_start(
            out=mr_d[slc:slc + 1, :], in_=VECS[0:1, 0:2 * S]),
            r=[vmb, vrb], w=[mrb[slc]])

        bcast_mr()
        for k in range(KC):
            hh = k % 2
            E("gpsimd", lambda e, k=k, t0=t0, hh=hh: e.tensor_tensor(
                POOLT[1][:, hh * 512:hh * 512 + S], X2[k][:, t0:t0 + S],
                MBR[0][:], ALU.subtract),
              r=[X2B[k][slc], mbrb[0]], w=[pt1h[hh]])
            E("vector", lambda e, hh=hh: e.tensor_tensor(
                POOLT[0][:, hh * 512:hh * 512 + S],
                POOLT[1][:, hh * 512:hh * 512 + S], MBR[1][:], ALU.mult),
              r=[pt1h[hh], mbrb[1]], w=[ptb[0]])
            E("vector", lambda e, hh=hh: e.tensor_reduce(
                BCV[:, BC_Z1:BC_Z1 + 1], POOLT[0][:, hh * 512:hh * 512 + S],
                AX, ALU.max, apply_absolute_value=True),
              r=[ptb[0]], w=[bcb[BC_Z1]])
            E("vector", lambda e: e.tensor_tensor(
                BCV[:, BC_LMAX:BC_LMAX + 1], BCV[:, BC_LMAX:BC_LMAX + 1],
                BCV[:, BC_Z1:BC_Z1 + 1], ALU.max),
              r=[bcb[BC_Z1]], w=[bcb[BC_LMAX]])

    DMA("gpsimd", lambda e: e.dma_start(
        out=cc1i_d[:], in_=BCV[:, BC_LMAX:BC_LMAX + 1]),
        r=[bcb[BC_LMAX]], w=[cc1ib])
    E("gpsimd", lambda e: e.collective_compute(
        "AllGather", ALU.bypass, replica_groups=RG,
        ins=[cc1i_d.ap().opt()], outs=[cc1o_d.ap().opt()]),
      r=[cc1ib], w=[cc1ob], sem=("cc", 1))

    XQ, xqb = AR.carve_n(64 * KB, KC, T * 2, BF16)
    xqsb = [[Buf() for _ in range(NSL)] for _ in range(KC)]

    def p2_stats(slc):
        DMA("sync", lambda e, slc=slc: e.dma_start(
            out=VECS[0:1, 0:2 * S], in_=mr_d[slc:slc + 1, :]),
            r=[mrb[slc]], w=[vmb, vrb])

    def part_reduce(col, dst):
        E("tensor", lambda e, col=col: e.transpose(
            PM[1][0:1, 0:128], BCV[:, col:col + 1], IDF[:]),
          r=[bcb[col], cb["idf"]], w=[pmb[1]])
        E("vector", lambda e, dst=dst: e.tensor_reduce(
            SCAL[0:1, dst:dst + 1], PM[1][0:1, 0:128], AX, ALU.max),
          r=[pmb[1]], w=[scb[dst]])

    def scale_chain(src_col, t_slot, s_slot, inv_slot):
        E("vector", lambda e: e.tensor_scalar(
            SCAL[0:1, t_slot:t_slot + 1], SCAL[0:1, src_col:src_col + 1],
            1e-6, None, ALU.max),
          r=[scb[src_col]], w=[scb[t_slot]])
        E("vector", lambda e: e.tensor_scalar(
            SCAL[0:1, s_slot:s_slot + 1], SCAL[0:1, t_slot:t_slot + 1],
            1.0 / 127.0, None, ALU.mult),
          r=[scb[t_slot]], w=[scb[s_slot]])
        E("vector", lambda e: e.reciprocal(
            SCAL[0:1, SC_T1:SC_T1 + 1], SCAL[0:1, t_slot:t_slot + 1]),
          r=[scb[t_slot]], w=[scb[SC_T1]])
        E("vector", lambda e: e.tensor_scalar(
            SCAL[0:1, inv_slot:inv_slot + 1], SCAL[0:1, SC_T1:SC_T1 + 1],
            127.0, None, ALU.mult),
          r=[scb[SC_T1]], w=[scb[inv_slot]])

    p2_stats(0)
    scan_reds(len(w1_tiles))
    part_reduce(BC_W1M, SC_W1M)
    scale_chain(SC_W1M, SC_T2, SC_SW1, SC_INVW1)
    bcast_scal(SC_INVW1, scb[SC_INVW1], BC_INVW1)

    WSP, wspb = AR.carve_n(96 * KB, 4, 2 * KB, F32)
    HEV, hevb = AR.carve_n(104 * KB, 4, 2 * KB, F32)
    W1Q, w1qb = AR.carve_n(114 * KB, 3, 2 * KB, BF16)
    wsp_ctr = [0]

    w2_pos = [0]
    hmax_pend = []
    w2scan_pos = [0]

    def emit_w2_scans(n):
        for _ in range(n):
            j = w2scan_pos[0]
            if j >= 64:
                return
            w2scan_pos[0] += 1
            i = wsp_ctr[0] % 4
            wsp_ctr[0] += 1
            DMA("sync" if j % 2 == 0 else "gpsimd",
                lambda e, j=j, i=i: e.dma_start(
                    out=WSP[i][:],
                    in_=w2T_d[j * 64:(j + 1) * 64, :].rearrange(
                        "a (b d) -> (a b) d", b=2)),
                w=[wspb[i]])
            E("vector", lambda e, i=i: e.tensor_reduce(
                BCV[:, BC_Z0:BC_Z0 + 1], WSP[i][:], AX, ALU.max,
                apply_absolute_value=True),
              r=[wspb[i]], w=[bcb[BC_Z0]])
            E("vector", lambda e: e.tensor_tensor(
                BCV[:, BC_W2M:BC_W2M + 1], BCV[:, BC_W2M:BC_W2M + 1],
                BCV[:, BC_Z0:BC_Z0 + 1], ALU.max),
              r=[bcb[BC_Z0]], w=[bcb[BC_W2M]])

    def emit_w2_round(n):
        for _ in range(n):
            if w2_pos[0] >= HC1:
                return
            j = w2_pos[0]
            w2_pos[0] += 1
            for hf in range(2):
                i = wsp_ctr[0] % 4
                wsp_ctr[0] += 1
                DMA("sync", lambda e, j=j, hf=hf, i=i: e.dma_start(
                    out=WSP[i][:], in_=w2T_d[j * 128:(j + 1) * 128,
                                             hf * 512:(hf + 1) * 512]),
                    w=[wspb[i]])
                E("scalar", lambda e, i=i: e.activation(
                    WSP[i][:], WSP[i][:], AF.Copy, bias=MAGIC,
                    scale=BCV[:, BC_INVW2:BC_INVW2 + 1]),
                  r=[wspb[i], bcb[BC_INVW2]], w=[wspb[i]])
                E("vector", lambda e, j=j, hf=hf, i=i: e.tensor_scalar(
                    W2Q[j][:, hf * 512:(hf + 1) * 512], WSP[i][:],
                    MAGIC, None, ALU.subtract),
                  r=[wspb[i]], w=[w2qb[j]])

    def w1_round(hc):
        iw = hc % 3
        for hf in range(2):
            i2 = wsp_ctr[0] % 4
            wsp_ctr[0] += 1
            DMA("sync", lambda e, hc=hc, hf=hf, i2=i2: e.dma_start(
                out=WSP[i2].rearrange("p (kc h) -> p kc h", h=128),
                in_=w1blk_d[hc, hf * 4:(hf + 1) * 4].rearrange("kc p h -> p kc h")),
                w=[wspb[i2]])
            E("scalar", lambda e, i2=i2: e.activation(
                WSP[i2][:], WSP[i2][:], AF.Copy, bias=MAGIC,
                scale=BCV[:, BC_INVW1:BC_INVW1 + 1]),
              r=[wspb[i2], bcb[BC_INVW1]], w=[wspb[i2]])
            E("vector", lambda e, hf=hf, i2=i2, iw=iw: e.tensor_scalar(
                W1Q[iw][:, hf * 512:(hf + 1) * 512], WSP[i2][:],
                MAGIC, None, ALU.subtract),
              r=[wspb[i2]], w=[w1qb[iw]])

    w1_round(0)
    w1_round(1)
    w1_round(2)

    gland = Buf()

    def land_gather(out_d, dst_col):
        n = 128 * NCORES
        nr = (n + 2 * S - 1) // (2 * S)
        w = n // nr
        for r in range(nr):
            DMA("gpsimd", lambda e, r=r, w=w: e.dma_start(
                out=VECS[32 * (r + 1):32 * (r + 1) + 1, 0:w],
                in_=out_d[r * w:(r + 1) * w].rearrange(
                    "(o p) one -> o (p one)", o=1)),
                r=[cc1ob if out_d is cc1o_d else cc2ob], w=[gland])
        for r in range(nr):
            tcol = dst_col if r == 0 else SC_T1
            E("vector", lambda e, r=r, w=w, tcol=tcol: e.tensor_reduce(
                SCAL[0:1, tcol:tcol + 1],
                VECS[32 * (r + 1):32 * (r + 1) + 1, 0:w], AX,
                ALU.max, apply_absolute_value=True),
              r=[gland], w=[scb[tcol]])
            if r > 0:
                E("vector", lambda e, dst_col=dst_col: e.tensor_tensor(
                    SCAL[0:1, dst_col:dst_col + 1],
                    SCAL[0:1, dst_col:dst_col + 1],
                    SCAL[0:1, SC_T1:SC_T1 + 1], ALU.max),
                  r=[scb[SC_T1]], w=[scb[dst_col]])

    pt0h = [Buf(carry=[ptb[0]]), Buf(carry=[ptb[0]])]
    psum_stage = [
        (PBW[0], 0, pwb[0][0]), (PBW[0], 512, pwb[0][1]),
        (PBW[1], 0, pwb[1][0]), (PBW[1], 512, pwb[1][1]),
        (PM[0], 0, pmb[0]), (CB[0], 0, cbb[0]), (CB[1], 0, cbb[1]),
    ]
    bcast_mr(eng="scalar")
    for k in range(KC):
        hh = k % 2
        E("gpsimd", lambda e, k=k, hh=hh: e.tensor_tensor(
            POOLT[1][:, hh * 512:hh * 512 + S], X2[k][:, 0:S],
            MBR[0][:], ALU.subtract),
          r=[X2B[k][0], mbrb[0]], w=[pt1h[hh]])
        if k < 7:
            bank, off, bbuf = psum_stage[k]
            E("vector", lambda e, bank=bank, off=off, hh=hh: e.tensor_tensor(
                bank[:, off:off + S],
                POOLT[1][:, hh * 512:hh * 512 + S], MBR[1][:], ALU.mult),
              r=[pt1h[hh], mbrb[1]], w=[bbuf])
        else:
            E("vector", lambda e, hh=hh: e.tensor_tensor(
                POOLT[0][:, hh * 512:hh * 512 + S],
                POOLT[1][:, hh * 512:hh * 512 + S], MBR[1][:], ALU.mult),
              r=[pt1h[hh], mbrb[1]], w=[pt0h[hh]])

    land_gather(cc1o_d, SC_AMX)
    scale_chain(SC_AMX, SC_TX, SC_SX, SC_INVX)
    bcast_scal(SC_INVX, scb[SC_INVX], BC_INVX)
    E("vector", lambda e: e.tensor_tensor(
        SCAL[0:1, SC_S1X:SC_S1X + 1], SCAL[0:1, SC_SW1:SC_SW1 + 1],
        SCAL[0:1, SC_SX:SC_SX + 1], ALU.mult),
      r=[scb[SC_SW1], scb[SC_SX]], w=[scb[SC_S1X]])
    bcast_scal(SC_S1X, scb[SC_S1X], BC_S1X)
    E("vector", lambda e: e.tensor_scalar(
        B1S[:], B1PP[:], BCV[:, BC_S1X:BC_S1X + 1], None, ALU.mult),
      r=[cb["b1pp"], bcb[BC_S1X]], w=[cb["b1s"]])

    for k in range(KC):
        hh = k % 2
        if k < 7:
            bank, off, bbuf = psum_stage[k]
            E("vector", lambda e, bank=bank, off=off, hh=hh: e.tensor_scalar(
                POOLT[1][:, hh * 512:hh * 512 + S], bank[:, off:off + S],
                BCV[:, BC_INVX:BC_INVX + 1], MAGIC, ALU.mult, ALU.add),
              r=[bbuf, bcb[BC_INVX]], w=[pt1h[hh]])
            E("scalar", lambda e, k=k, hh=hh: e.activation(
                XQ[k][:, 0:S], POOLT[1][:, hh * 512:hh * 512 + S],
                AF.Copy, bias=-MAGIC, scale=1.0),
              r=[pt1h[hh]], w=[xqb[k], xqsb[k][0]])
        else:
            E("vector", lambda e, hh=hh: e.tensor_scalar(
                POOLT[0][:, hh * 512:hh * 512 + S],
                POOLT[0][:, hh * 512:hh * 512 + S],
                BCV[:, BC_INVX:BC_INVX + 1], MAGIC, ALU.mult, ALU.add),
              r=[bcb[BC_INVX]], w=[pt0h[hh]])
            E("scalar", lambda e, k=k, hh=hh: e.activation(
                XQ[k][:, 0:S], POOLT[0][:, hh * 512:hh * 512 + S],
                AF.Copy, bias=-MAGIC, scale=1.0),
              r=[pt0h[hh]], w=[xqb[k], xqsb[k][0]])

    def xqpass(slc):
        t0 = slc * S
        bcast_mr(eng="scalar")
        for k in range(KC):
            hh = k % 2
            E("gpsimd", lambda e, k=k, t0=t0, hh=hh: e.tensor_tensor(
                POOLT[1][:, hh * 512:hh * 512 + S], X2[k][:, t0:t0 + S],
                MBR[0][:], ALU.subtract),
              r=[X2B[k][slc], mbrb[0]], w=[pt1h[hh]])
            E("vector", lambda e, hh=hh: e.tensor_tensor(
                POOLT[0][:, hh * 512:hh * 512 + S],
                POOLT[1][:, hh * 512:hh * 512 + S], MBR[1][:], ALU.mult),
              r=[pt1h[hh], mbrb[1]], w=[pt0h[hh]])
            E("vector", lambda e, hh=hh: e.tensor_scalar(
                POOLT[0][:, hh * 512:hh * 512 + S],
                POOLT[0][:, hh * 512:hh * 512 + S],
                BCV[:, BC_INVX:BC_INVX + 1], MAGIC, ALU.mult, ALU.add),
              r=[bcb[BC_INVX]], w=[pt0h[hh]])
            E("scalar", lambda e, k=k, t0=t0, hh=hh: e.activation(
                XQ[k][:, t0:t0 + S], POOLT[0][:, hh * 512:hh * 512 + S],
                AF.Copy, bias=-MAGIC, scale=1.0),
              r=[pt0h[hh]], w=[xqb[k], xqsb[k][slc]])

    if NSL > 1:
        p2_stats(1)
    for slc in range(1, NSL):
        xqpass(slc)
        if slc + 1 < NSL:
            p2_stats(slc + 1)
    for b in pt0h:
        if b.w is not None:
            ptb[0].r.append(b.w)
        ptb[0].r.extend(b.r)
    emit_w2_scans(12)

    W2Q, w2qb = AR.carve_n(0, HC1, 2 * KB, BF16)

    for hc in range(HC1):
        iw = hc % 3
        for th in range(NTH2):
            pb_i = (hc * NTH2 + th) % 2
            W512 = min(512, THW)
            mm1_tok = E("tensor", lambda e, i=iw, th=th, pb_i=pb_i, W512=W512: [
                e.matmul(PBW[pb_i][:, half * W512:(half + 1) * W512],
                         W1Q[i][:, k * 128:(k + 1) * 128],
                         XQ[k][:, th * THW + half * W512:th * THW + (half + 1) * W512],
                         start=(k == 0), stop=(k == KC - 1))
                for k in range(KC) for half in range(THW // W512)][-1],
              r=[w1qb[iw]] + [xqsb[k][th * (THW // S) + j]
                              for k in range(KC) for j in range(THW // S)],
              w=[pwb[pb_i][0], pwb[pb_i][1]])
            for k in range(KC):
                xqb[k].r.append(mm1_tok)
            WH = min(512, THW)
            NHLF = THW // WH
            NTCH = WH // 128
            for hf2 in range(NHLF):
                hv = (NHLF * (hc * NTH2 + th) + hf2) % 4
                E("scalar", lambda e, hc=hc, th=th, hf2=hf2, hv=hv, pb_i=pb_i,
                    WH=WH: e.activation(
                        HEV[hv][:, 0:WH], PBW[pb_i][:, hf2 * WH:(hf2 + 1) * WH],
                        AF.Gelu, bias=B1S[:, hc:hc + 1],
                        scale=BCV[:, BC_S1X:BC_S1X + 1]),
                  r=[pwb[pb_i][hf2 * WH // 512], cb["b1s"], bcb[BC_S1X]],
                  w=[hevb[hv]])
                tc0 = th * (THW // 128) + hf2 * NTCH
                DMA("gpsimd", lambda e, hc=hc, tc0=tc0, hv=hv, NTCH=NTCH,
                    WH=WH: e.dma_start(
                    out=hsp_d[tc0:tc0 + NTCH, hc, :, :].rearrange("tc p t -> p tc t"),
                    in_=HEV[hv][:, 0:WH].rearrange("p (tc t) -> p tc t", t=128)),
                    r=[hevb[hv]], w=[hspb[t] for t in range(tc0, tc0 + NTCH)])
                hmax_pend.append(hv)
                if len(hmax_pend) > 2:
                    hv_o = hmax_pend.pop(0)
                    E("vector", lambda e, hv_o=hv_o, WH=WH: e.tensor_reduce(
                        BCV[:, BC_Z1:BC_Z1 + 1], HEV[hv_o][:, 0:WH], AX, ALU.max,
                        apply_absolute_value=True),
                      r=[hevb[hv_o]], w=[bcb[BC_Z1]])
                    E("vector", lambda e: e.tensor_tensor(
                        BCV[:, BC_HMAX:BC_HMAX + 1],
                        BCV[:, BC_HMAX:BC_HMAX + 1],
                        BCV[:, BC_Z1:BC_Z1 + 1], ALU.max),
                      r=[bcb[BC_Z1]], w=[bcb[BC_HMAX]])
        if hc + 3 < HC1:
            w1_round(hc + 3)
        if hc < 11:
            emit_w2_scans(6)
        else:
            if hc == 11:
                emit_w2_scans(64)
                part_reduce(BC_W2M, SC_W2M)
                scale_chain(SC_W2M, SC_T2, SC_SW2, SC_INVW2)
                bcast_scal(SC_INVW2, scb[SC_INVW2], BC_INVW2)
            emit_w2_round(2)
    WHF = min(512, THW)
    while hmax_pend:
        pb_o = hmax_pend.pop(0)
        E("vector", lambda e, pb_o=pb_o: e.tensor_reduce(
            BCV[:, BC_Z1:BC_Z1 + 1], HEV[pb_o][:, 0:WHF], AX, ALU.max,
            apply_absolute_value=True),
          r=[hevb[pb_o]], w=[bcb[BC_Z1]])
        E("vector", lambda e: e.tensor_tensor(
            BCV[:, BC_HMAX:BC_HMAX + 1], BCV[:, BC_HMAX:BC_HMAX + 1],
            BCV[:, BC_Z1:BC_Z1 + 1], ALU.max),
          r=[bcb[BC_Z1]], w=[bcb[BC_HMAX]])
    emit_w2_round(HC1)

    DMA("gpsimd", lambda e: e.dma_start(
        out=cc2i_d[:], in_=BCV[:, BC_HMAX:BC_HMAX + 1]),
        r=[bcb[BC_HMAX]], w=[cc2ib])
    E("gpsimd", lambda e: e.collective_compute(
        "AllGather", ALU.bypass, replica_groups=RG,
        ins=[cc2i_d.ap().opt()], outs=[cc2o_d.ap().opt()]),
      r=[cc2ib], w=[cc2ob], sem=("cc", 1), extra=[("cc", 1)])
    HT, htb = AR.carve_n(64 * KB, 2, 16 * KB, F32, ncols=4096)
    HQ, hqb = AR.carve_n(96 * KB, 2, 8 * KB, BF16, ncols=4096)
    X2TOK_l, x2tb_l = AR.carve_n(112 * KB, 1, 4 * KB, F32, ncols=1024)
    X2TOK, x2tb = X2TOK_l[0], x2tb_l[0]
    outdb = Buf()

    X2TOK2_l, x2tb2_l = AR.carve_n(116 * KB, 1, 4 * KB, F32, ncols=1024)
    X2TOKS = [X2TOK, X2TOK2_l[0]]
    x2tbs = [x2tb, x2tb2_l[0]]

    def p4_ht_load(tc):
        i = tc % 2
        DMA("sync", lambda e, tc=tc, i=i: e.dma_start(
            out=HT[i].rearrange("p (kc t) -> p kc t", t=128),
            in_=hsp_d[tc].rearrange("kc p t -> p kc t")),
            r=[hspb[tc]], w=[htb[i]])

    hqcb = [[Buf() for _ in range(4)] for _ in range(2)]

    def p4_hq_round(tc):
        i = tc % 2
        for c in range(4):
            W4 = min(1024, (T * 2) // 1)
            E("scalar", lambda e, i=i, c=c: e.activation(
                HT[i][:, c * 1024:(c + 1) * 1024],
                HT[i][:, c * 1024:(c + 1) * 1024], AF.Copy, bias=MAGIC,
                scale=BCV[:, BC_INVH:BC_INVH + 1]),
              r=[htb[i], bcb[BC_INVH]], w=[htb[i]])
            E("vector", lambda e, i=i, c=c: e.tensor_scalar(
                HQ[i][:, c * 1024:(c + 1) * 1024],
                HT[i][:, c * 1024:(c + 1) * 1024], MAGIC, None, ALU.subtract),
              r=[htb[i]], w=[hqb[i], hqcb[i][c]])

    def p4_load_round(tc):
        p4_ht_load(tc)
        p4_hq_round(tc)

    p4_ht_load(0)
    if NTC > 1:
        p4_ht_load(1)

    def p4_transposes(tc):
        j = tc % 2
        slc = tc * 128 // S
        for dc in range(KC):
            pj = dc % 2
            sl4 = (dc % 4) // 2
            E("tensor", lambda e, dc=dc, tc=tc, pj=pj, sl4=sl4: e.transpose(
                PM[pj][:, 128 * sl4:128 * sl4 + 128],
                X2[dc][:, tc * 128:(tc + 1) * 128], IDF[:]),
              r=[X2B[dc][slc], cb["idf"]], w=[pmb[pj]])
            E("vector", lambda e, dc=dc, pj=pj, sl4=sl4, j=j: e.tensor_copy(
                X2TOKS[j][:, dc * 128:(dc + 1) * 128],
                PM[pj][:, 128 * sl4:128 * sl4 + 128]),
              r=[pmb[pj]], w=[x2tbs[j]])

    p4_transposes(0)
    if NTC > 1:
        p4_transposes(1)

    land_gather(cc2o_d, SC_AMH)
    scale_chain(SC_AMH, SC_TH, SC_SH, SC_INVH)
    bcast_scal(SC_INVH, scb[SC_INVH], BC_INVH)
    E("vector", lambda e: e.tensor_tensor(
        SCAL[0:1, SC_S2X:SC_S2X + 1], SCAL[0:1, SC_SW2:SC_SW2 + 1],
        SCAL[0:1, SC_SH:SC_SH + 1], ALU.mult),
      r=[scb[SC_SW2], scb[SC_SH]], w=[scb[SC_S2X]])
    bcast_scal(SC_S2X, scb[SC_S2X], BC_S2X)


    def p4_mm2(tc):
        i = tc % 2
        for c in range(4):
            def mm2_chunk(e, i=i, c=c):
                last = None
                for k in range(c * 8, (c + 1) * 8):
                    for half in range(2):
                        last = e.matmul(
                            PBW[i][:, half * 512:(half + 1) * 512],
                            HQ[i][:, k * 128:(k + 1) * 128],
                            W2Q[k][:, half * 512:(half + 1) * 512],
                            start=(k == 0), stop=False)
                if c == 3:
                    for half in range(2):
                        last = e.matmul(
                            PBW[i][:, half * 512:(half + 1) * 512], ONESB[:],
                            B2R[0:1, half * 512:(half + 1) * 512],
                            start=False, stop=True)
                return last
            E("tensor", mm2_chunk,
              r=[hqcb[i][c], cb["onesb"], cb["b2r"]]
                + [w2qb[k] for k in range(c * 8, (c + 1) * 8)],
              w=[pwb[i][0], pwb[i][1]])

    def p4_finish(tc):
        i = tc % 2
        j = tc % 2
        E("vector", lambda e, i=i, j=j: e.scalar_tensor_tensor(
            POOLT[i][:], PBW[i][:, 0:1024], BCV[:, BC_S2X:BC_S2X + 1],
            X2TOKS[j][:], ALU.mult, ALU.add),
          r=[pwb[i][0], pwb[i][1], bcb[BC_S2X], x2tbs[j]], w=[ptb[i]])
        DMA("gpsimd", lambda e, tc=tc, i=i: e.dma_start(
            out=out_d[tc * 128:(tc + 1) * 128, :], in_=POOLT[i][:]),
            r=[ptb[i]], w=[outdb])

    p4_hq_round(0)
    if NTC > 1:
        p4_hq_round(1)
    for tc in range(NTC):
        p4_mm2(tc)
        if tc + 2 < NTC:
            p4_ht_load(tc + 2)
            p4_hq_round(tc + 2)
        p4_finish(tc)
        if tc + 2 < NTC:
            p4_transposes(tc + 2)

    pr.finish()
    return nc


def _prep(inputs, T=2048):
    f32 = np.float32
    bf = ml_dtypes.bfloat16
    x = np.asarray(inputs["L0_attn_x"], f32).reshape(-1, D)
    w_qkv = np.asarray(inputs["L0_attn_w_qkv"], f32)
    w_out = np.asarray(inputs["L0_attn_w_out"], f32)
    b_out = np.asarray(inputs["L0_attn_b_out"], f32)
    proj = np.asarray(inputs["L0_attn_proj"], f32)
    w1 = np.asarray(inputs["L0_ffn_w1"], f32)
    b1 = np.asarray(inputs["L0_ffn_b1"], f32)
    w2 = np.asarray(inputs["L0_ffn_w2"], f32)
    b2 = np.asarray(inputs["L0_ffn_b2"], f32)

    shared = {
        "wqkvT": np.ascontiguousarray(w_qkv.T),
        "woutT": np.ascontiguousarray(w_out.T),
        "projE": np.ascontiguousarray(proj.transpose(1, 0, 2).reshape(128, NH * 128)),
        "boutpp": np.ascontiguousarray(b_out.reshape(8, 128).T),
        "b1pp": np.ascontiguousarray(b1.reshape(32, 128).T),
        "b2row": np.ascontiguousarray(b2.reshape(1, D)),
        "w1blk": np.ascontiguousarray(
            w1.T.reshape(KC, 128, HC1, 128).transpose(2, 0, 1, 3)),
        "w2T": np.ascontiguousarray(w2.T),
        "idf": np.eye(128, dtype=f32),
        "idb": np.eye(128, dtype=f32).astype(bf),
        "onesr": np.ones((1, 128), f32),
        "onesb": np.ones((1, 128), f32).astype(bf),
        "invd": np.full((128, 1), 1.0 / 1024.0, f32).astype(bf),
        "cvec": np.broadcast_to(
            np.array([12582912.0, 1e-5, -0.5, 0.0], f32), (128, 4)).copy(),
    }
    n_sh = x.shape[0] // T
    in_maps = []
    for c in range(n_sh):
        m = dict(shared)
        m["xT"] = np.ascontiguousarray(x[c * T:(c + 1) * T, :].T)
        in_maps.append(m)
    return in_maps


def kernel(**inputs):
    T = 2048
    nc = build(T)
    in_maps = _prep(inputs, T)
    res = run_bass_kernel_spmd(nc, in_maps, core_ids=list(range(NCORES)))
    shards = [np.asarray(res.results[c]["out"]) for c in range(NCORES)]
    full = np.concatenate(shards, axis=0).reshape(4, 4096, D)
    return full.astype(np.float32)


# version 56
# speedup vs baseline: 1.0931x; 1.0931x over previous
import numpy as np
import ml_dtypes
import concourse.bass as bass
import concourse.mybir as mybir
from concourse.bass_utils import run_bass_kernel_spmd

F32 = mybir.dt.float32
BF16 = mybir.dt.bfloat16
U8 = mybir.dt.uint8
AF = mybir.ActivationFunctionType
ALU = mybir.AluOpType
AX = mybir.AxisListType.X
MAGIC = 12582912.0
NCORES = 8
D = 1024
NH = 8
KC = 8
HC1 = 32
KB = 1024


class Buf:
    __slots__ = ("w", "r")

    def __init__(self, carry=()):
        self.w = None
        self.r = []
        for b in carry:
            if b.w is not None:
                self.r.append(b.w)
            self.r.extend(b.r)


class Prog:
    ENGS = ("tensor", "vector", "scalar", "gpsimd", "sync")
    NRING = 8

    def __init__(self, nc):
        self.nc = nc
        names = ["pe", "dve", "act", "pool", "cc"]
        names += [f"ds{i}" for i in range(self.NRING)]
        names += [f"dg{i}" for i in range(self.NRING)]
        self.sem = {n: nc.alloc_semaphore("s_" + n) for n in names}
        self.cnt = {n: 0 for n in names}
        self.items = {e: [] for e in self.ENGS}
        self.waited = {e: {n: 0 for n in names} for e in self.ENGS}
        self.rr = {"sync": 0, "gpsimd": 0}

    def emit(self, eng, fn, r=(), w=(), sem=None, extra=()):
        if sem is None:
            sem = {"tensor": ("pe", 1), "vector": ("dve", 1),
                   "scalar": ("act", 1), "gpsimd": ("pool", 1)}[eng]
        sname, amt = sem
        deps = {}

        def dep(tok):
            if tok is not None:
                s, v = tok
                if deps.get(s, 0) < v:
                    deps[s] = v
        for b in r:
            dep(b.w)
        for b in w:
            dep(b.w)
            for t in b.r:
                dep(t)
        for t in extra:
            dep(t)
        own = {"tensor": "pe", "vector": "dve", "scalar": "act",
               "gpsimd": "pool"}.get(eng)
        waits = []
        for s, v in deps.items():
            if s == own or self.waited[eng][s] < v:
                waits.append((s, v))
                if self.waited[eng][s] < v:
                    self.waited[eng][s] = v
        self.cnt[sname] += amt
        tok = (sname, self.cnt[sname])
        self.items[eng].append((waits, fn, self.sem[sname], amt))
        for b in r:
            b.r.append(tok)
        for b in w:
            b.w = tok
            b.r = []
        return tok

    def dma(self, queue, fn, r=(), w=(), extra=()):
        pfx = "ds" if queue == "sync" else "dg"
        slot = self.rr[queue] % self.NRING
        self.rr[queue] += 1
        sname = f"{pfx}{slot}"
        gate = (sname, self.cnt[sname]) if self.cnt[sname] else None
        extra = list(extra) + ([gate] if gate else [])
        return self.emit(queue, fn, r=r, w=w, sem=(sname, 16), extra=extra)

    def finish(self):
        finals = dict(self.cnt)
        with self.nc.Block() as block:
            for eng_name in self.ENGS:
                def make(engname):
                    def body(eng):
                        for waits, fn, semh, amt in self.items[engname]:
                            for s, v in waits:
                                eng.wait_ge(self.sem[s], v)
                            inst = fn(eng)
                            inst.then_inc(semh, amt)
                        for s, v in finals.items():
                            if v > self.waited[engname][s]:
                                eng.wait_ge(self.sem[s], v)
                    return body
                getattr(block, eng_name)(make(eng_name))


class Region:
    def __init__(self, handle, nbytes):
        self.h = handle
        self.nbytes = nbytes
        self.carves = []

    def carve(self, start, nbytes, dtype, ncols=None):
        assert start % 32 == 0 and start + nbytes <= self.nbytes, (start, nbytes)
        ap = self.h.ap()[:, start:start + nbytes]
        if dtype != U8:
            ap = ap.bitcast(dtype)
        if ncols is not None:
            ap = ap[:, 0:ncols]
        carry = [b for s, e, b in self.carves if s < start + nbytes and start < e]
        buf = Buf(carry)
        self.carves.append((start, start + nbytes, buf))
        return ap, buf

    def carve_n(self, start, n, step, dtype, ncols=None):
        aps, bufs = [], []
        for i in range(n):
            ap, b = self.carve(start + i * step, step, dtype, ncols)
            aps.append(ap)
            bufs.append(b)
        return aps, bufs


def build(T=2048):
    S = min(512, T)
    NSL = T // S
    TCS = S // 128
    NTC = T // 128
    THW = min(1024, T)
    NTH2 = T // THW
    TQW = 256
    assert T % S == 0 and S % 256 == 0

    nc = bass.Bass("TRN2", target_bir_lowering=False)

    xT = nc.dram_tensor("xT", [D, T], F32, kind="ExternalInput")
    wqkvT_d = nc.dram_tensor("wqkvT", [D, 3 * D], F32, kind="ExternalInput")
    woutT_d = nc.dram_tensor("woutT", [D, D], F32, kind="ExternalInput")
    projE_d = nc.dram_tensor("projE", [128, NH * 128], F32, kind="ExternalInput")
    boutpp_d = nc.dram_tensor("boutpp", [128, 8], F32, kind="ExternalInput")
    b1pp_d = nc.dram_tensor("b1pp", [128, 32], F32, kind="ExternalInput")
    b2row_d = nc.dram_tensor("b2row", [1, D], F32, kind="ExternalInput")
    w1blk_d = nc.dram_tensor("w1blk", [HC1, KC, 128, 128], F32, kind="ExternalInput")
    w2T_d = nc.dram_tensor("w2T", [4096, D], F32, kind="ExternalInput")
    idf_d = nc.dram_tensor("idf", [128, 128], F32, kind="ExternalInput")
    idb_d = nc.dram_tensor("idb", [128, 128], BF16, kind="ExternalInput")
    onesr_d = nc.dram_tensor("onesr", [1, 128], F32, kind="ExternalInput")
    onesb_d = nc.dram_tensor("onesb", [1, 128], BF16, kind="ExternalInput")
    invd_d = nc.dram_tensor("invd", [128, 1], BF16, kind="ExternalInput")
    cvec_d = nc.dram_tensor("cvec", [128, 4], F32, kind="ExternalInput")
    out_d = nc.dram_tensor("out", [T, D], F32, kind="ExternalOutput")

    hsp_d = nc.dram_tensor("hsp", [NTC, HC1, 128, 128], F32)
    hspb = [Buf() for _ in range(NTC)]
    mr_d = nc.dram_tensor("mrsp", [NSL, 2 * S], F32)
    mrb = [Buf() for _ in range(NSL)]
    cc1i_d = nc.dram_tensor("cc1i", [128, 1], F32)
    cc1o_d = nc.dram_tensor("cc1o", [128 * NCORES, 1], F32)
    cc2i_d = nc.dram_tensor("cc2i", [128, 1], F32)
    cc2o_d = nc.dram_tensor("cc2o", [128 * NCORES, 1], F32)
    cc1ib, cc1ob, cc2ib, cc2ob = Buf(), Buf(), Buf(), Buf()
    RG = [list(range(NCORES))]

    def sbt(name, shape, dt):
        return nc.alloc_sbuf_tensor(name, shape, dt)

    X2 = [sbt(f"x2_{k}", [128, T], F32) for k in range(KC)]
    X2B = [[Buf() for _ in range(NSL)] for _ in range(KC)]
    VECS = sbt("vecs", [128, 2 * S], F32)
    vmb, vrb = Buf(), Buf()
    SCAL = sbt("scal", [128, 64], F32)
    (SC_AMX, SC_TX, SC_SX, SC_SW1, SC_S1X, SC_AMH, SC_TH, SC_SH, SC_SW2,
     SC_S2X, SC_W1M, SC_W2M, SC_T1, SC_T2, SC_INVX, SC_INVW1, SC_INVH,
     SC_INVW2) = range(18)
    scb = [Buf() for _ in range(20)]
    BCV = sbt("bcv", [128, 16], F32)
    (BC_INVX, BC_S1X, BC_INVW1, BC_INVH, BC_S2X, BC_INVW2, BC_LMAX, BC_W1M,
     BC_W2M, BC_HMAX, BC_Z0, BC_Z1) = range(12)
    bcb = [Buf() for _ in range(16)]
    POOLT = [sbt(f"poolt{i}", [128, 1024], F32) for i in range(2)]
    ptb = [Buf(), Buf()]
    pt1h = [Buf(), Buf()]
    EXPT = [sbt(f"expt{i}", [128, 512], F32) for i in range(2)]
    exb = [Buf(), Buf()]
    STSB = [sbt(f"stsb{i}", [128, 64], BF16) for i in range(2)]
    stb = [Buf() for _ in range(2)]
    ATOK = [sbt(f"atok{i}", [128, 128], F32) for i in range(4)]
    atb = [Buf() for _ in range(4)]
    ZV = sbt("zv", [128, 8], F32)
    zvb = [Buf(), Buf()]
    IDF = sbt("idf_s", [128, 128], F32)
    IDB = sbt("idb_s", [128, 128], BF16)
    ONESR = sbt("onesr_s", [1, 128], F32)
    ONESB = sbt("onesb_s", [1, 128], BF16)
    INVD = sbt("invd_s", [128, 1], BF16)
    B1PP = sbt("b1pp_s", [128, 32], F32)
    B1S = sbt("b1s_s", [128, 32], F32)
    BOUT = sbt("bout_s", [128, 8], F32)
    B2R = sbt("b2r_s", [1, D], BF16)
    CV = sbt("cv_s", [128, 4], F32)
    cb = {n: Buf() for n in ("idf", "idb", "onesr", "onesb", "invd", "b1pp",
                             "b1s", "bout", "b2r", "proj", "cv")}

    AR_BYTES = 120 * KB
    ar_h = sbt("arena", [128, AR_BYTES], U8)
    AR = Region(ar_h, AR_BYTES)

    WQKV, wqkvb = AR.carve_n(0, KC, 6 * KB, BF16)
    WOUTS, woutb = AR.carve_n(48 * KB, KC, 2 * KB, BF16)
    LN1S, ln1b = AR.carve_n(64 * KB, KC, 1 * KB, BF16, ncols=S)
    QF, qfb = AR.carve_n(72 * KB, 2, 1 * KB, BF16, ncols=S)
    KF, kfb = AR.carve_n(74 * KB, 2, 1 * KB, BF16, ncols=S)
    QP, qpb = AR.carve_n(76 * KB, 2, 1 * KB, BF16, ncols=S)
    KP, kpb = AR.carve_n(78 * KB, 2, 1 * KB, BF16, ncols=S)
    AFM, afb = AR.carve_n(80 * KB, KC, 1 * KB, BF16, ncols=S)
    VT, vtb = AR.carve_n(88 * KB, TCS, 2080, BF16, ncols=1032)
    XB16, xbb = AR.carve_n(97 * KB, KC, 1 * KB, BF16, ncols=S)
    SCANB, scnb = AR.carve_n(105 * KB, 2, 2 * KB, F32)
    MBR, mbrb = AR.carve_n(109 * KB, 2, 2 * KB, F32, ncols=S)
    SQB, sqb = AR.carve_n(113 * KB, 4, 512, BF16, ncols=TQW)
    PROJS_l, projb_l = AR.carve_n(115 * KB, 1, 2 * KB, BF16)
    PROJS = PROJS_l[0]
    cb["proj"] = projb_l[0]

    PBW = [nc.alloc_psum_tensor(f"pbw{i}", [128, 1024], F32) for i in range(2)]
    pwb = [[Buf(), Buf()], [Buf(), Buf()]]
    PM = [nc.alloc_psum_tensor(f"pm{i}", [128, 512], F32) for i in range(2)]
    pmb = [Buf(), Buf()]
    CB = [nc.alloc_psum_tensor(f"cb{i}", [128, 512], F32) for i in range(2)]
    cbb = [Buf(), Buf()]
    CB_ST = [CB[i][:, 0:64] for i in range(2)]
    CB_NUM = [[CB[i][:, 64:193], CB[i][:, 200:329]] for i in range(2)]
    CB_DEN2 = [CB[i].rearrange("p (a b) -> p a b", b=1)[:, :, 0]
               for i in range(2)]
    CB_TR = [CB[i][:, 336:464] for i in range(2)]

    pr = Prog(nc)
    E = pr.emit
    DMA = pr.dma

    DMA("sync", lambda e: e.dma_start(out=IDF[:], in_=idf_d[:]), w=[cb["idf"]])
    DMA("sync", lambda e: e.dma_start(out=IDB[:], in_=idb_d[:]), w=[cb["idb"]])
    DMA("sync", lambda e: e.dma_start(out=ONESR[:], in_=onesr_d[:]), w=[cb["onesr"]])
    DMA("sync", lambda e: e.dma_start(out=ONESB[:], in_=onesb_d[:]), w=[cb["onesb"]])
    DMA("sync", lambda e: e.dma_start(out=INVD[:], in_=invd_d[:]), w=[cb["invd"]])
    DMA("sync", lambda e: e.dma_start(out=B1PP[:], in_=b1pp_d[:]), w=[cb["b1pp"]])
    DMA("sync", lambda e: e.dma_start(out=BOUT[:], in_=boutpp_d[:]), w=[cb["bout"]])
    DMA("sync", lambda e: e.dma_start(out=CV[:], in_=cvec_d[:]), w=[cb["cv"]])
    DMA("gpsimd", lambda e: e.dma_start(out=B2R[:], in_=b2row_d[:]), w=[cb["b2r"]])
    DMA("gpsimd", lambda e: e.dma_start(out=PROJS[:], in_=projE_d[:]), w=[cb["proj"]])
    for k in range(KC):
        DMA("sync", lambda e, k=k: e.dma_start(
            out=X2[k][:, 0:S], in_=xT[k * 128:(k + 1) * 128, 0:S]),
            w=[X2B[k][0]])
    vcast_copies = []
    if T >= 2048:
        for k in range(KC):
            DMA("sync", lambda e, k=k: e.dma_start(
                out=X2[k][:, 512:1536],
                in_=wqkvT_d[k * 128:(k + 1) * 128, 2048:3072]),
                w=[X2B[k][1], X2B[k][2]])

            def _cp(k=k):
                E("scalar", lambda e, k=k: e.copy(
                    WQKV[k][:, 2048:3072], X2[k][:, 512:1536]),
                  r=[X2B[k][1], X2B[k][2]], w=[wqkvb[k]])
            vcast_copies.append(_cp)
        parts = ((0, 1024), (1024, 2048))
    else:
        parts = ((2048, 3072), (0, 1024), (1024, 2048))
    for part0, part1 in parts:
        for k in range(KC):
            DMA("gpsimd", lambda e, k=k, part0=part0, part1=part1: e.dma_start(
                out=WQKV[k][:, part0:part1],
                in_=wqkvT_d[k * 128:(k + 1) * 128, part0:part1]),
                w=[wqkvb[k]])
    for k in range(KC):
        DMA("gpsimd", lambda e, k=k: e.dma_start(
            out=WOUTS[k][:], in_=woutT_d[k * 128:(k + 1) * 128, :]), w=[woutb[k]])
    for tc in range(TCS):
        E("vector", lambda e, tc=tc: e.memset(VT[tc][:], 1.0), w=[vtb[tc]])
    for c in (BC_W1M, BC_W2M, BC_LMAX, BC_HMAX):
        E("vector", lambda e, c=c: e.memset(BCV[:, c:c + 1], 0.0), w=[bcb[c]])

    w1_tiles = [(hc, hf) for hc in range(HC1) for hf in range(2)]
    sc_dma = [0]
    sc_red = [0]

    def scan_dmas(n):
        for _ in range(n):
            j = sc_dma[0]
            if j >= len(w1_tiles):
                return
            sc_dma[0] += 1
            hc, hf = w1_tiles[j]
            i = j % 2
            src_ap = w1blk_d[hc, hf * 4:(hf + 1) * 4].rearrange("kc p h -> p kc h")
            DMA("gpsimd", lambda e, src_ap=src_ap, i=i: e.dma_start(
                out=SCANB[i].rearrange("p (kc h) -> p kc h", h=128), in_=src_ap),
                w=[scnb[i]])

    def scan_reds(n):
        for _ in range(n):
            if sc_red[0] >= sc_dma[0] or sc_red[0] >= len(w1_tiles):
                return
            j = sc_red[0]
            sc_red[0] += 1
            i = j % 2
            E("vector", lambda e, i=i: e.tensor_reduce(
                BCV[:, BC_Z0:BC_Z0 + 1], SCANB[i][:], AX, ALU.max,
                apply_absolute_value=True),
              r=[scnb[i]], w=[bcb[BC_Z0]])
            E("vector", lambda e: e.tensor_tensor(
                BCV[:, BC_W1M:BC_W1M + 1], BCV[:, BC_W1M:BC_W1M + 1],
                BCV[:, BC_Z0:BC_Z0 + 1], ALU.max),
              r=[bcb[BC_Z0]], w=[bcb[BC_W1M]])

    def scan_step():
        if sc_red[0] + 2 <= sc_dma[0]:
            scan_reds(2)
        scan_dmas(2)

    def ln_stats(src_aps, src_bufs):
        for tq in range(S // TQW):
            E("tensor", lambda e, tq=tq: [
                e.matmul(PM[0][0:1, 0:TQW], INVD[:],
                         src_aps[k][:, tq * TQW:(tq + 1) * TQW],
                         start=(k == 0), stop=(k == KC - 1))
                for k in range(KC)][-1],
              r=[cb["invd"]] + list(src_bufs), w=[pmb[0]])
            for k in range(KC):
                i = k % 4
                E("vector", lambda e, k=k, i=i, tq=tq: e.tensor_tensor(
                    SQB[i][:], src_aps[k][:, tq * TQW:(tq + 1) * TQW],
                    src_aps[k][:, tq * TQW:(tq + 1) * TQW], ALU.mult),
                  r=[src_bufs[k]], w=[sqb[i]])
                if k % 4 == 3:
                    E("tensor", lambda e, k=k: [
                        e.matmul(PM[0][0:1, TQW:2 * TQW], INVD[:], SQB[j][:],
                                 start=(k - 3 + j == 0),
                                 stop=(k - 3 + j == KC - 1))
                        for j in range(4)][-1],
                      r=[cb["invd"]] + [sqb[j] for j in range(4)], w=[pmb[0]])
            c0 = tq * TQW
            E("vector", lambda e, c0=c0: e.tensor_copy(
                VECS[0:1, c0:c0 + TQW], PM[0][0:1, 0:TQW]),
              r=[pmb[0]], w=[vmb])
            E("vector", lambda e, c0=c0: e.tensor_tensor(
                POOLT[0][0:1, 0:TQW], VECS[0:1, c0:c0 + TQW],
                VECS[0:1, c0:c0 + TQW], ALU.mult),
              r=[vmb], w=[ptb[0]])
            E("vector", lambda e: e.tensor_tensor(
                POOLT[0][0:1, 256:256 + TQW], PM[0][0:1, TQW:2 * TQW],
                POOLT[0][0:1, 0:TQW], ALU.subtract),
              r=[pmb[0], ptb[0]], w=[ptb[0]])
            E("scalar", lambda e: e.activation(
                POOLT[0][0:1, 0:TQW], POOLT[0][0:1, 256:256 + TQW], AF.Ln,
                bias=CV[0:1, 1:2]),
              r=[ptb[0], cb["cv"]], w=[ptb[0]])
            E("scalar", lambda e, c0=c0: e.activation(
                VECS[0:1, S + c0:S + c0 + TQW], POOLT[0][0:1, 0:TQW], AF.Exp,
                scale=CV[0:1, 2:3]),
              r=[ptb[0], cb["cv"]], w=[vrb])

    def bcast_mr(eng="vector"):
        def cp(dst, srcap):
            if eng == "vector":
                E("vector", lambda e, dst=dst, srcap=srcap: e.tensor_copy(
                    dst[0][:], srcap), r=[pmb[1]], w=[dst[1]])
            else:
                E("scalar", lambda e, dst=dst, srcap=srcap: e.copy(
                    dst[0][:], srcap), r=[pmb[1]], w=[dst[1]])
        E("tensor", lambda e: e.matmul(
            PM[1][:, 0:S], ONESR[:], VECS[0:1, 0:S], start=True, stop=True),
          r=[cb["onesr"], vmb], w=[pmb[1]])
        cp((MBR[0], mbrb[0]), PM[1][:, 0:S])
        E("tensor", lambda e: e.matmul(
            PM[1][:, 0:S], ONESR[:], VECS[0:1, S:2 * S], start=True, stop=True),
          r=[cb["onesr"], vrb], w=[pmb[1]])
        cp((MBR[1], mbrb[1]), PM[1][:, 0:S])

    def bcast_scal(col, sbuf_, bc):
        E("tensor", lambda e, col=col: e.matmul(
            PM[1][:, 0:1], ONESR[:], SCAL[0:1, col:col + 1],
            start=True, stop=True),
          r=[cb["onesr"], sbuf_], w=[pmb[1]])
        E("vector", lambda e, bc=bc: e.tensor_copy(
            BCV[:, bc:bc + 1], PM[1][:, 0:1]),
          r=[pmb[1]], w=[bcb[bc]])

    slot_ctr = [0]
    _slots = ([(PBW[i][:, j * 512:(j + 1) * 512], pwb[i][j])
               for i in range(2) for j in range(2)]
              + [(PM[i][:, 0:512], pmb[i]) for i in range(2)])

    def next_slot():
        n = slot_ctr[0] % len(_slots)
        slot_ctr[0] += 1
        return _slots[n]

    def x2_load(slc):
        t0 = slc * S
        for k in range(KC):
            DMA("sync", lambda e, k=k, t0=t0: e.dma_start(
                out=X2[k][:, t0:t0 + S],
                in_=xT[k * 128:(k + 1) * 128, t0:t0 + S]),
                w=[X2B[k][slc]])

    def stage_ln_stats(slc):
        t0 = slc * S
        for k in range(KC):
            E("scalar", lambda e, k=k, t0=t0: e.copy(
                XB16[k][:], X2[k][:, t0:t0 + S]),
              r=[X2B[k][slc]], w=[xbb[k]])
        ln_stats([XB16[k] for k in range(KC)], xbb)
        bcast_mr()

    def stage_ln_norm(slc):
        t0 = slc * S
        for k in range(KC):
            hh = k % 2
            E("gpsimd", lambda e, k=k, t0=t0, hh=hh: e.tensor_tensor(
                POOLT[1][:, hh * 512:hh * 512 + S], X2[k][:, t0:t0 + S],
                MBR[0][:], ALU.subtract),
              r=[X2B[k][slc], mbrb[0]], w=[pt1h[hh]])
            E("vector", lambda e, k=k, hh=hh: e.tensor_tensor(
                LN1S[k][:], POOLT[1][:, hh * 512:hh * 512 + S], MBR[1][:],
                ALU.mult),
              r=[pt1h[hh], mbrb[1]], w=[ln1b[k]])

    stage_ln_stats(0)
    stage_ln_norm(0)
    for cp in vcast_copies:
        cp()
    for slc in range(NSL):
        t0 = slc * S
        if slc + 1 < NSL:
            x2_load(slc + 1)
        for tc in range(TCS):
            for half in range(2):
                ps, psb = next_slot()
                if tc == 0:
                    for k in range(KC):
                        E("tensor", lambda e, tc=tc, half=half, ps=ps, k=k:
                            e.matmul(ps, LN1S[k][:, tc * 128:(tc + 1) * 128],
                                     WQKV[k][:, 2048 + half * 512:
                                              2048 + (half + 1) * 512],
                                     start=(k == 0), stop=(k == KC - 1)),
                          r=[ln1b[k], wqkvb[k]], w=[psb])
                else:
                    E("tensor", lambda e, tc=tc, half=half, ps=ps: [
                        e.matmul(ps, LN1S[k][:, tc * 128:(tc + 1) * 128],
                                 WQKV[k][:, 2048 + half * 512:
                                          2048 + (half + 1) * 512],
                                 start=(k == 0), stop=(k == KC - 1))
                        for k in range(KC)][-1],
                      r=[ln1b[kk] for kk in range(KC)]
                        + [wqkvb[kk] for kk in range(KC)],
                      w=[psb])
                E("vector", lambda e, tc=tc, half=half, ps=ps: e.tensor_copy(
                    VT[tc].rearrange("p (h c) -> p h c", c=129)
                        [:, 4 * half:4 * half + 4, 0:128],
                    ps.rearrange("p (h t) -> p h t", t=128)),
                  r=[psb], w=[vtb[tc]])

        def qk_group(wq_oc, h, dst_aps, dst_bufs):
            hb = h % 2
            ps, psb = next_slot()
            E("tensor", lambda e, wq_oc=wq_oc, ps=ps: [
                e.matmul(ps[:, 0:S], WQKV[k][:, wq_oc * 128:(wq_oc + 1) * 128],
                         LN1S[k][:], start=(k == 0), stop=(k == KC - 1))
                for k in range(KC)][-1],
              r=[ln1b[kk] for kk in range(KC)] + [wqkvb[kk] for kk in range(KC)],
              w=[psb])
            E("scalar", lambda e, hb=hb, ps=ps: e.copy(
                dst_aps[hb][:], ps[:, 0:S]),
              r=[psb], w=[dst_bufs[hb]])

        def proj_elu(h, dst_aps, dst_bufs, pdst_aps, pdst_bufs, eb):
            hb = h % 2
            ps2, psb2 = next_slot()
            E("tensor", lambda e, h=h, hb=hb, ps2=ps2: e.matmul(
                ps2[:, 0:S], PROJS[:, h * 128:(h + 1) * 128], dst_aps[hb][:],
                start=True, stop=True),
              r=[cb["proj"], dst_bufs[hb]], w=[psb2])
            E("scalar", lambda e, ps2=ps2, eb=eb: e.activation(
                EXPT[eb][:, 0:S], ps2[:, 0:S], AF.Exp),
              r=[psb2], w=[exb[eb]])
            E("vector", lambda e, eb=eb: e.tensor_scalar(
                EXPT[eb][:, 0:S], EXPT[eb][:, 0:S], 1.0, None, ALU.min),
              r=[exb[eb]], w=[exb[eb]])
            E("vector", lambda e, hb=hb, ps2=ps2, eb=eb: e.scalar_tensor_tensor(
                pdst_aps[hb][:], ps2[:, 0:S], 0.0, EXPT[eb][:, 0:S],
                ALU.max, ALU.add),
              r=[psb2, exb[eb]], w=[pdst_bufs[hb]])

        def chains_part1_st(h):
            hb = h % 2
            for b in range(TCS // 2):
                E("tensor", lambda e, h=h, b=b, hb=hb: [
                    e.matmul(CB_ST[b][32 * c:32 * (c + 1), 32 * t:32 * (t + 1)],
                             KP[hb][:, (2 * b + t) * 128 + 32 * c:
                                      (2 * b + t) * 128 + 32 * (c + 1)],
                             QP[hb][:, (2 * b + t) * 128 + 32 * c:
                                      (2 * b + t) * 128 + 32 * (c + 1)],
                             start=True, stop=True, tile_position=(0, 32 * c))
                    for t in range(2) for c in range(4)][-1],
                  r=[kpb[hb], qpb[hb]], w=[cbb[b]])
                E("scalar", lambda e, b=b: e.copy(STSB[b][:], CB_ST[b]),
                  r=[cbb[b]], w=[stb[b]])

        def chains_part1_num(h):
            for b in range(TCS // 2):
                E("tensor", lambda e, h=h, b=b: [
                    e.matmul(CB_NUM[b][t][32 * c:32 * (c + 1), :],
                             STSB[b][32 * c:32 * (c + 1), 32 * t:32 * (t + 1)],
                             VT[2 * b + t][32 * c:32 * (c + 1),
                                           h * 129:(h + 1) * 129],
                             start=True, stop=True, tile_position=(32 * c, 32 * c))
                    for t in range(2) for c in range(4)][-1],
                  r=[stb[b], vtb[2 * b], vtb[2 * b + 1]], w=[cbb[b]])

        def chains_part1(h):
            chains_part1_st(h)
            chains_part1_num(h)

        def chains_part2z(h):
            for b in range(TCS // 2):
                for t in range(2):
                    E("vector", lambda e, b=b, t=t: e.tensor_scalar(
                        ZV[:, 4 * b + t:4 * b + t + 1],
                        CB_NUM[b][t][:, 128:129], 1e-8, None, ALU.add),
                      r=[cbb[b]], w=[zvb[b]])
                E("vector", lambda e, b=b: e.reciprocal(
                    ZV[:, 4 * b + 2:4 * b + 4], ZV[:, 4 * b:4 * b + 2]),
                  r=[zvb[b]], w=[zvb[b]])
                for t in range(2):
                    E("vector", lambda e, b=b, t=t: e.tensor_scalar(
                        ATOK[2 * b + t][:], CB_NUM[b][t][:, 0:128],
                        ZV[:, 4 * b + 2 + t:4 * b + 3 + t], None, ALU.mult),
                      r=[cbb[b], zvb[b]], w=[atb[2 * b + t]])

        def chains_part2t(h):
            for t in range(2):
                for b in range(TCS // 2):
                    E("tensor", lambda e, b=b, t=t: e.transpose(
                        CB_TR[b], ATOK[2 * b + t][:], IDF[:]),
                      r=[atb[2 * b + t], cb["idf"]], w=[cbb[b]])
                    E("scalar", lambda e, h=h, b=b, t=t: e.copy(
                        AFM[h][:, (2 * b + t) * 128:(2 * b + t + 1) * 128],
                        CB_TR[b]),
                      r=[cbb[b]], w=[afb[h]])

        pend = None
        pend_t = None
        for h in range(NH):
            qk_group(h, h, QF, qfb)
            if pend_t is not None:
                chains_part2t(pend_t)
                pend_t = None
            if pend is not None:
                chains_part1_st(pend)
            qk_group(NH + h, h, KF, kfb)
            if pend is not None:
                chains_part1_num(pend)
                chains_part2z(pend)
            proj_elu(h, QF, qfb, QP, qpb, 0)
            proj_elu(h, KF, kfb, KP, kpb, 1)
            if pend is not None:
                pend_t = pend
            pend = h
            scan_step()
        if pend_t is not None:
            chains_part2t(pend_t)
        chains_part1(pend)
        chains_part2z(pend)
        chains_part2t(pend)

        for oc in range(KC):
            ps, psb = next_slot()
            E("tensor", lambda e, oc=oc, ps=ps: [
                e.matmul(ps[:, 0:S], WOUTS[k][:, oc * 128:(oc + 1) * 128],
                         AFM[k][:], start=(k == 0), stop=(k == KC - 1))
                for k in range(KC)][-1],
              r=[afb[kk] for kk in range(KC)] + [woutb[kk] for kk in range(KC)],
              w=[psb])
            E("vector", lambda e, oc=oc, ps=ps, t0=t0: e.scalar_tensor_tensor(
                X2[oc][:, t0:t0 + S], ps[:, 0:S], BOUT[:, oc:oc + 1],
                X2[oc][:, t0:t0 + S], ALU.add, ALU.add),
              r=[psb, cb["bout"], X2B[oc][slc]], w=[X2B[oc][slc]])

        if slc + 1 < NSL:
            stage_ln_stats(slc + 1)
            stage_ln_norm(slc + 1)

        for k in range(KC):
            E("scalar", lambda e, k=k, t0=t0: e.copy(
                XB16[k][:], X2[k][:, t0:t0 + S]),
              r=[X2B[k][slc]], w=[xbb[k]])
        ln_stats([XB16[k] for k in range(KC)], xbb)
        DMA("sync", lambda e, slc=slc: e.dma_start(
            out=mr_d[slc:slc + 1, :], in_=VECS[0:1, 0:2 * S]),
            r=[vmb, vrb], w=[mrb[slc]])

        bcast_mr()
        for k in range(KC):
            hh = k % 2
            E("gpsimd", lambda e, k=k, t0=t0, hh=hh: e.tensor_tensor(
                POOLT[1][:, hh * 512:hh * 512 + S], X2[k][:, t0:t0 + S],
                MBR[0][:], ALU.subtract),
              r=[X2B[k][slc], mbrb[0]], w=[pt1h[hh]])
            E("vector", lambda e, hh=hh: e.tensor_tensor(
                POOLT[0][:, hh * 512:hh * 512 + S],
                POOLT[1][:, hh * 512:hh * 512 + S], MBR[1][:], ALU.mult),
              r=[pt1h[hh], mbrb[1]], w=[ptb[0]])
            E("vector", lambda e, hh=hh: e.tensor_reduce(
                BCV[:, BC_Z1:BC_Z1 + 1], POOLT[0][:, hh * 512:hh * 512 + S],
                AX, ALU.max, apply_absolute_value=True),
              r=[ptb[0]], w=[bcb[BC_Z1]])
            E("vector", lambda e: e.tensor_tensor(
                BCV[:, BC_LMAX:BC_LMAX + 1], BCV[:, BC_LMAX:BC_LMAX + 1],
                BCV[:, BC_Z1:BC_Z1 + 1], ALU.max),
              r=[bcb[BC_Z1]], w=[bcb[BC_LMAX]])

    DMA("gpsimd", lambda e: e.dma_start(
        out=cc1i_d[:], in_=BCV[:, BC_LMAX:BC_LMAX + 1]),
        r=[bcb[BC_LMAX]], w=[cc1ib])
    E("gpsimd", lambda e: e.collective_compute(
        "AllGather", ALU.bypass, replica_groups=RG,
        ins=[cc1i_d.ap().opt()], outs=[cc1o_d.ap().opt()]),
      r=[cc1ib], w=[cc1ob], sem=("cc", 1))

    XQ, xqb = AR.carve_n(64 * KB, KC, T * 2, BF16)
    xqsb = [[Buf() for _ in range(NSL)] for _ in range(KC)]

    def p2_stats(slc):
        DMA("sync", lambda e, slc=slc: e.dma_start(
            out=VECS[0:1, 0:2 * S], in_=mr_d[slc:slc + 1, :]),
            r=[mrb[slc]], w=[vmb, vrb])

    def part_reduce(col, dst):
        E("tensor", lambda e, col=col: e.transpose(
            PM[1][0:1, 0:128], BCV[:, col:col + 1], IDF[:]),
          r=[bcb[col], cb["idf"]], w=[pmb[1]])
        E("vector", lambda e, dst=dst: e.tensor_reduce(
            SCAL[0:1, dst:dst + 1], PM[1][0:1, 0:128], AX, ALU.max),
          r=[pmb[1]], w=[scb[dst]])

    def scale_chain(src_col, t_slot, s_slot, inv_slot):
        E("vector", lambda e: e.tensor_scalar(
            SCAL[0:1, t_slot:t_slot + 1], SCAL[0:1, src_col:src_col + 1],
            1e-6, None, ALU.max),
          r=[scb[src_col]], w=[scb[t_slot]])
        E("vector", lambda e: e.tensor_scalar(
            SCAL[0:1, s_slot:s_slot + 1], SCAL[0:1, t_slot:t_slot + 1],
            1.0 / 127.0, None, ALU.mult),
          r=[scb[t_slot]], w=[scb[s_slot]])
        E("vector", lambda e: e.reciprocal(
            SCAL[0:1, SC_T1:SC_T1 + 1], SCAL[0:1, t_slot:t_slot + 1]),
          r=[scb[t_slot]], w=[scb[SC_T1]])
        E("vector", lambda e: e.tensor_scalar(
            SCAL[0:1, inv_slot:inv_slot + 1], SCAL[0:1, SC_T1:SC_T1 + 1],
            127.0, None, ALU.mult),
          r=[scb[SC_T1]], w=[scb[inv_slot]])

    p2_stats(0)
    scan_reds(len(w1_tiles))
    part_reduce(BC_W1M, SC_W1M)
    scale_chain(SC_W1M, SC_T2, SC_SW1, SC_INVW1)
    bcast_scal(SC_INVW1, scb[SC_INVW1], BC_INVW1)

    WSP, wspb = AR.carve_n(96 * KB, 4, 2 * KB, F32)
    HEV, hevb = AR.carve_n(104 * KB, 4, 2 * KB, F32)
    W1Q, w1qb = AR.carve_n(114 * KB, 3, 2 * KB, BF16)
    wsp_ctr = [0]

    w2_pos = [0]
    hmax_pend = []
    w2scan_pos = [0]

    def emit_w2_scans(n):
        for _ in range(n):
            j = w2scan_pos[0]
            if j >= 64:
                return
            w2scan_pos[0] += 1
            i = wsp_ctr[0] % 4
            wsp_ctr[0] += 1
            DMA("sync" if j % 2 == 0 else "gpsimd",
                lambda e, j=j, i=i: e.dma_start(
                    out=WSP[i][:],
                    in_=w2T_d[j * 64:(j + 1) * 64, :].rearrange(
                        "a (b d) -> (a b) d", b=2)),
                w=[wspb[i]])
            E("vector", lambda e, i=i: e.tensor_reduce(
                BCV[:, BC_Z0:BC_Z0 + 1], WSP[i][:], AX, ALU.max,
                apply_absolute_value=True),
              r=[wspb[i]], w=[bcb[BC_Z0]])
            E("vector", lambda e: e.tensor_tensor(
                BCV[:, BC_W2M:BC_W2M + 1], BCV[:, BC_W2M:BC_W2M + 1],
                BCV[:, BC_Z0:BC_Z0 + 1], ALU.max),
              r=[bcb[BC_Z0]], w=[bcb[BC_W2M]])

    def emit_w2_round(n):
        for _ in range(n):
            if w2_pos[0] >= HC1:
                return
            j = w2_pos[0]
            w2_pos[0] += 1
            for hf in range(2):
                i = wsp_ctr[0] % 4
                wsp_ctr[0] += 1
                DMA("sync", lambda e, j=j, hf=hf, i=i: e.dma_start(
                    out=WSP[i][:], in_=w2T_d[j * 128:(j + 1) * 128,
                                             hf * 512:(hf + 1) * 512]),
                    w=[wspb[i]])
                E("scalar", lambda e, i=i: e.activation(
                    WSP[i][:], WSP[i][:], AF.Copy, bias=MAGIC,
                    scale=BCV[:, BC_INVW2:BC_INVW2 + 1]),
                  r=[wspb[i], bcb[BC_INVW2]], w=[wspb[i]])
                E("vector", lambda e, j=j, hf=hf, i=i: e.tensor_scalar(
                    W2Q[j][:, hf * 512:(hf + 1) * 512], WSP[i][:],
                    MAGIC, None, ALU.subtract),
                  r=[wspb[i]], w=[w2qb[j]])

    def w1_round(hc):
        iw = hc % 3
        for hf in range(2):
            i2 = wsp_ctr[0] % 4
            wsp_ctr[0] += 1
            DMA("sync", lambda e, hc=hc, hf=hf, i2=i2: e.dma_start(
                out=WSP[i2].rearrange("p (kc h) -> p kc h", h=128),
                in_=w1blk_d[hc, hf * 4:(hf + 1) * 4].rearrange("kc p h -> p kc h")),
                w=[wspb[i2]])
            E("scalar", lambda e, i2=i2: e.activation(
                WSP[i2][:], WSP[i2][:], AF.Copy, bias=MAGIC,
                scale=BCV[:, BC_INVW1:BC_INVW1 + 1]),
              r=[wspb[i2], bcb[BC_INVW1]], w=[wspb[i2]])
            E("vector", lambda e, hf=hf, i2=i2, iw=iw: e.tensor_scalar(
                W1Q[iw][:, hf * 512:(hf + 1) * 512], WSP[i2][:],
                MAGIC, None, ALU.subtract),
              r=[wspb[i2]], w=[w1qb[iw]])

    w1_round(0)
    w1_round(1)
    w1_round(2)

    gland = Buf()

    def land_gather(out_d, dst_col):
        n = 128 * NCORES
        nr = (n + 2 * S - 1) // (2 * S)
        w = n // nr
        for r in range(nr):
            DMA("gpsimd", lambda e, r=r, w=w: e.dma_start(
                out=VECS[32 * (r + 1):32 * (r + 1) + 1, 0:w],
                in_=out_d[r * w:(r + 1) * w].rearrange(
                    "(o p) one -> o (p one)", o=1)),
                r=[cc1ob if out_d is cc1o_d else cc2ob], w=[gland])
        for r in range(nr):
            tcol = dst_col if r == 0 else SC_T1
            E("vector", lambda e, r=r, w=w, tcol=tcol: e.tensor_reduce(
                SCAL[0:1, tcol:tcol + 1],
                VECS[32 * (r + 1):32 * (r + 1) + 1, 0:w], AX,
                ALU.max, apply_absolute_value=True),
              r=[gland], w=[scb[tcol]])
            if r > 0:
                E("vector", lambda e, dst_col=dst_col: e.tensor_tensor(
                    SCAL[0:1, dst_col:dst_col + 1],
                    SCAL[0:1, dst_col:dst_col + 1],
                    SCAL[0:1, SC_T1:SC_T1 + 1], ALU.max),
                  r=[scb[SC_T1]], w=[scb[dst_col]])

    land_gather(cc1o_d, SC_AMX)
    scale_chain(SC_AMX, SC_TX, SC_SX, SC_INVX)
    bcast_scal(SC_INVX, scb[SC_INVX], BC_INVX)
    E("vector", lambda e: e.tensor_tensor(
        SCAL[0:1, SC_S1X:SC_S1X + 1], SCAL[0:1, SC_SW1:SC_SW1 + 1],
        SCAL[0:1, SC_SX:SC_SX + 1], ALU.mult),
      r=[scb[SC_SW1], scb[SC_SX]], w=[scb[SC_S1X]])
    bcast_scal(SC_S1X, scb[SC_S1X], BC_S1X)
    E("vector", lambda e: e.tensor_scalar(
        B1S[:], B1PP[:], BCV[:, BC_S1X:BC_S1X + 1], None, ALU.mult),
      r=[cb["b1pp"], bcb[BC_S1X]], w=[cb["b1s"]])

    pt0h = [Buf(carry=[ptb[0]]), Buf(carry=[ptb[0]])]

    def xqpass(slc):
        t0 = slc * S
        bcast_mr(eng="scalar")
        for k in range(KC):
            hh = k % 2
            E("gpsimd", lambda e, k=k, t0=t0, hh=hh: e.tensor_tensor(
                POOLT[1][:, hh * 512:hh * 512 + S], X2[k][:, t0:t0 + S],
                MBR[0][:], ALU.subtract),
              r=[X2B[k][slc], mbrb[0]], w=[pt1h[hh]])
            mult_eng = "gpsimd" if slc < 2 else "vector"
            E(mult_eng, lambda e, hh=hh: e.tensor_tensor(
                POOLT[0][:, hh * 512:hh * 512 + S],
                POOLT[1][:, hh * 512:hh * 512 + S], MBR[1][:], ALU.mult),
              r=[pt1h[hh], mbrb[1]], w=[pt0h[hh]])
            E("vector", lambda e, hh=hh: e.tensor_scalar(
                POOLT[0][:, hh * 512:hh * 512 + S],
                POOLT[0][:, hh * 512:hh * 512 + S],
                BCV[:, BC_INVX:BC_INVX + 1], MAGIC, ALU.mult, ALU.add),
              r=[bcb[BC_INVX]], w=[pt0h[hh]])
            E("scalar", lambda e, k=k, t0=t0, hh=hh: e.activation(
                XQ[k][:, t0:t0 + S], POOLT[0][:, hh * 512:hh * 512 + S],
                AF.Copy, bias=-MAGIC, scale=1.0),
              r=[pt0h[hh]], w=[xqb[k], xqsb[k][slc]])

    for slc in range(NSL):
        xqpass(slc)
        if slc + 1 < NSL:
            p2_stats(slc + 1)
    for b in pt0h:
        if b.w is not None:
            ptb[0].r.append(b.w)
        ptb[0].r.extend(b.r)
    emit_w2_scans(12)

    W2Q, w2qb = AR.carve_n(0, HC1, 2 * KB, BF16)

    for hc in range(HC1):
        iw = hc % 3
        for th in range(NTH2):
            pb_i = (hc * NTH2 + th) % 2
            W512 = min(512, THW)
            mm1_tok = E("tensor", lambda e, i=iw, th=th, pb_i=pb_i, W512=W512: [
                e.matmul(PBW[pb_i][:, half * W512:(half + 1) * W512],
                         W1Q[i][:, k * 128:(k + 1) * 128],
                         XQ[k][:, th * THW + half * W512:th * THW + (half + 1) * W512],
                         start=(k == 0), stop=(k == KC - 1))
                for k in range(KC) for half in range(THW // W512)][-1],
              r=[w1qb[iw]] + [xqsb[k][th * (THW // S) + j]
                              for k in range(KC) for j in range(THW // S)],
              w=[pwb[pb_i][0], pwb[pb_i][1]])
            for k in range(KC):
                xqb[k].r.append(mm1_tok)
            WH = min(512, THW)
            NHLF = THW // WH
            NTCH = WH // 128
            for hf2 in range(NHLF):
                hv = (NHLF * (hc * NTH2 + th) + hf2) % 4
                E("scalar", lambda e, hc=hc, th=th, hf2=hf2, hv=hv, pb_i=pb_i,
                    WH=WH: e.activation(
                        HEV[hv][:, 0:WH], PBW[pb_i][:, hf2 * WH:(hf2 + 1) * WH],
                        AF.Gelu, bias=B1S[:, hc:hc + 1],
                        scale=BCV[:, BC_S1X:BC_S1X + 1]),
                  r=[pwb[pb_i][hf2 * WH // 512], cb["b1s"], bcb[BC_S1X]],
                  w=[hevb[hv]])
                tc0 = th * (THW // 128) + hf2 * NTCH
                DMA("gpsimd", lambda e, hc=hc, tc0=tc0, hv=hv, NTCH=NTCH,
                    WH=WH: e.dma_start(
                    out=hsp_d[tc0:tc0 + NTCH, hc, :, :].rearrange("tc p t -> p tc t"),
                    in_=HEV[hv][:, 0:WH].rearrange("p (tc t) -> p tc t", t=128)),
                    r=[hevb[hv]], w=[hspb[t] for t in range(tc0, tc0 + NTCH)])
                hmax_pend.append(hv)
                if len(hmax_pend) > 2:
                    hv_o = hmax_pend.pop(0)
                    E("vector", lambda e, hv_o=hv_o, WH=WH: e.tensor_reduce(
                        BCV[:, BC_Z1:BC_Z1 + 1], HEV[hv_o][:, 0:WH], AX, ALU.max,
                        apply_absolute_value=True),
                      r=[hevb[hv_o]], w=[bcb[BC_Z1]])
                    E("vector", lambda e: e.tensor_tensor(
                        BCV[:, BC_HMAX:BC_HMAX + 1],
                        BCV[:, BC_HMAX:BC_HMAX + 1],
                        BCV[:, BC_Z1:BC_Z1 + 1], ALU.max),
                      r=[bcb[BC_Z1]], w=[bcb[BC_HMAX]])
        if hc + 3 < HC1:
            w1_round(hc + 3)
        if hc < 11:
            emit_w2_scans(6)
        else:
            if hc == 11:
                emit_w2_scans(64)
                part_reduce(BC_W2M, SC_W2M)
                scale_chain(SC_W2M, SC_T2, SC_SW2, SC_INVW2)
                bcast_scal(SC_INVW2, scb[SC_INVW2], BC_INVW2)
            emit_w2_round(2)
    WHF = min(512, THW)
    while hmax_pend:
        pb_o = hmax_pend.pop(0)
        E("vector", lambda e, pb_o=pb_o: e.tensor_reduce(
            BCV[:, BC_Z1:BC_Z1 + 1], HEV[pb_o][:, 0:WHF], AX, ALU.max,
            apply_absolute_value=True),
          r=[hevb[pb_o]], w=[bcb[BC_Z1]])
        E("vector", lambda e: e.tensor_tensor(
            BCV[:, BC_HMAX:BC_HMAX + 1], BCV[:, BC_HMAX:BC_HMAX + 1],
            BCV[:, BC_Z1:BC_Z1 + 1], ALU.max),
          r=[bcb[BC_Z1]], w=[bcb[BC_HMAX]])
    emit_w2_round(HC1)

    DMA("gpsimd", lambda e: e.dma_start(
        out=cc2i_d[:], in_=BCV[:, BC_HMAX:BC_HMAX + 1]),
        r=[bcb[BC_HMAX]], w=[cc2ib])
    E("gpsimd", lambda e: e.collective_compute(
        "AllGather", ALU.bypass, replica_groups=RG,
        ins=[cc2i_d.ap().opt()], outs=[cc2o_d.ap().opt()]),
      r=[cc2ib], w=[cc2ob], sem=("cc", 1), extra=[("cc", 1)])
    HT, htb = AR.carve_n(64 * KB, 2, 16 * KB, F32, ncols=4096)
    HQ, hqb = AR.carve_n(96 * KB, 2, 8 * KB, BF16, ncols=4096)
    X2TOK_l, x2tb_l = AR.carve_n(112 * KB, 1, 4 * KB, F32, ncols=1024)
    X2TOK, x2tb = X2TOK_l[0], x2tb_l[0]
    outdb = Buf()

    X2TOK2_l, x2tb2_l = AR.carve_n(116 * KB, 1, 4 * KB, F32, ncols=1024)
    X2TOKS = [X2TOK, X2TOK2_l[0]]
    x2tbs = [x2tb, x2tb2_l[0]]

    def p4_ht_load(tc):
        i = tc % 2
        DMA("sync", lambda e, tc=tc, i=i: e.dma_start(
            out=HT[i].rearrange("p (kc t) -> p kc t", t=128),
            in_=hsp_d[tc].rearrange("kc p t -> p kc t")),
            r=[hspb[tc]], w=[htb[i]])

    hqcb = [[Buf() for _ in range(4)] for _ in range(2)]

    def p4_hq_round(tc):
        i = tc % 2
        for c in range(4):
            W4 = min(1024, (T * 2) // 1)
            E("scalar", lambda e, i=i, c=c: e.activation(
                HT[i][:, c * 1024:(c + 1) * 1024],
                HT[i][:, c * 1024:(c + 1) * 1024], AF.Copy, bias=MAGIC,
                scale=BCV[:, BC_INVH:BC_INVH + 1]),
              r=[htb[i], bcb[BC_INVH]], w=[htb[i]])
            E("vector", lambda e, i=i, c=c: e.tensor_scalar(
                HQ[i][:, c * 1024:(c + 1) * 1024],
                HT[i][:, c * 1024:(c + 1) * 1024], MAGIC, None, ALU.subtract),
              r=[htb[i]], w=[hqb[i], hqcb[i][c]])

    def p4_load_round(tc):
        p4_ht_load(tc)
        p4_hq_round(tc)

    p4_ht_load(0)
    if NTC > 1:
        p4_ht_load(1)

    def p4_transposes(tc):
        j = tc % 2
        slc = tc * 128 // S
        for dc in range(KC):
            pj = dc % 2
            sl4 = (dc % 4) // 2
            E("tensor", lambda e, dc=dc, tc=tc, pj=pj, sl4=sl4: e.transpose(
                PM[pj][:, 128 * sl4:128 * sl4 + 128],
                X2[dc][:, tc * 128:(tc + 1) * 128], IDF[:]),
              r=[X2B[dc][slc], cb["idf"]], w=[pmb[pj]])
            E("vector", lambda e, dc=dc, pj=pj, sl4=sl4, j=j: e.tensor_copy(
                X2TOKS[j][:, dc * 128:(dc + 1) * 128],
                PM[pj][:, 128 * sl4:128 * sl4 + 128]),
              r=[pmb[pj]], w=[x2tbs[j]])

    p4_transposes(0)
    if NTC > 1:
        p4_transposes(1)

    land_gather(cc2o_d, SC_AMH)
    scale_chain(SC_AMH, SC_TH, SC_SH, SC_INVH)
    bcast_scal(SC_INVH, scb[SC_INVH], BC_INVH)
    E("vector", lambda e: e.tensor_tensor(
        SCAL[0:1, SC_S2X:SC_S2X + 1], SCAL[0:1, SC_SW2:SC_SW2 + 1],
        SCAL[0:1, SC_SH:SC_SH + 1], ALU.mult),
      r=[scb[SC_SW2], scb[SC_SH]], w=[scb[SC_S2X]])
    bcast_scal(SC_S2X, scb[SC_S2X], BC_S2X)


    def p4_mm2(tc):
        i = tc % 2
        for c in range(4):
            def mm2_chunk(e, i=i, c=c):
                last = None
                for k in range(c * 8, (c + 1) * 8):
                    for half in range(2):
                        last = e.matmul(
                            PBW[i][:, half * 512:(half + 1) * 512],
                            HQ[i][:, k * 128:(k + 1) * 128],
                            W2Q[k][:, half * 512:(half + 1) * 512],
                            start=(k == 0), stop=False)
                if c == 3:
                    for half in range(2):
                        last = e.matmul(
                            PBW[i][:, half * 512:(half + 1) * 512], ONESB[:],
                            B2R[0:1, half * 512:(half + 1) * 512],
                            start=False, stop=True)
                return last
            E("tensor", mm2_chunk,
              r=[hqcb[i][c], cb["onesb"], cb["b2r"]]
                + [w2qb[k] for k in range(c * 8, (c + 1) * 8)],
              w=[pwb[i][0], pwb[i][1]])

    def p4_finish(tc):
        i = tc % 2
        j = tc % 2
        E("vector", lambda e, i=i, j=j: e.scalar_tensor_tensor(
            POOLT[i][:], PBW[i][:, 0:1024], BCV[:, BC_S2X:BC_S2X + 1],
            X2TOKS[j][:], ALU.mult, ALU.add),
          r=[pwb[i][0], pwb[i][1], bcb[BC_S2X], x2tbs[j]], w=[ptb[i]])
        DMA("gpsimd", lambda e, tc=tc, i=i: e.dma_start(
            out=out_d[tc * 128:(tc + 1) * 128, :], in_=POOLT[i][:]),
            r=[ptb[i]], w=[outdb])

    p4_hq_round(0)
    if NTC > 1:
        p4_hq_round(1)
    for tc in range(NTC):
        p4_mm2(tc)
        if tc + 2 < NTC:
            p4_ht_load(tc + 2)
            p4_hq_round(tc + 2)
        p4_finish(tc)
        if tc + 2 < NTC:
            p4_transposes(tc + 2)

    pr.finish()
    return nc


def _prep(inputs, T=2048):
    f32 = np.float32
    bf = ml_dtypes.bfloat16
    x = np.asarray(inputs["L0_attn_x"], f32).reshape(-1, D)
    w_qkv = np.asarray(inputs["L0_attn_w_qkv"], f32)
    w_out = np.asarray(inputs["L0_attn_w_out"], f32)
    b_out = np.asarray(inputs["L0_attn_b_out"], f32)
    proj = np.asarray(inputs["L0_attn_proj"], f32)
    w1 = np.asarray(inputs["L0_ffn_w1"], f32)
    b1 = np.asarray(inputs["L0_ffn_b1"], f32)
    w2 = np.asarray(inputs["L0_ffn_w2"], f32)
    b2 = np.asarray(inputs["L0_ffn_b2"], f32)

    shared = {
        "wqkvT": np.ascontiguousarray(w_qkv.T),
        "woutT": np.ascontiguousarray(w_out.T),
        "projE": np.ascontiguousarray(proj.transpose(1, 0, 2).reshape(128, NH * 128)),
        "boutpp": np.ascontiguousarray(b_out.reshape(8, 128).T),
        "b1pp": np.ascontiguousarray(b1.reshape(32, 128).T),
        "b2row": np.ascontiguousarray(b2.reshape(1, D)),
        "w1blk": np.ascontiguousarray(
            w1.T.reshape(KC, 128, HC1, 128).transpose(2, 0, 1, 3)),
        "w2T": np.ascontiguousarray(w2.T),
        "idf": np.eye(128, dtype=f32),
        "idb": np.eye(128, dtype=f32).astype(bf),
        "onesr": np.ones((1, 128), f32),
        "onesb": np.ones((1, 128), f32).astype(bf),
        "invd": np.full((128, 1), 1.0 / 1024.0, f32).astype(bf),
        "cvec": np.broadcast_to(
            np.array([12582912.0, 1e-5, -0.5, 0.0], f32), (128, 4)).copy(),
    }
    n_sh = x.shape[0] // T
    in_maps = []
    for c in range(n_sh):
        m = dict(shared)
        m["xT"] = np.ascontiguousarray(x[c * T:(c + 1) * T, :].T)
        in_maps.append(m)
    return in_maps


def kernel(**inputs):
    T = 2048
    nc = build(T)
    in_maps = _prep(inputs, T)
    res = run_bass_kernel_spmd(nc, in_maps, core_ids=list(range(NCORES)))
    shards = [np.asarray(res.results[c]["out"]) for c in range(NCORES)]
    full = np.concatenate(shards, axis=0).reshape(4, 4096, D)
    return full.astype(np.float32)


# version 57
# speedup vs baseline: 1.1065x; 1.0123x over previous
import numpy as np
import ml_dtypes
import concourse.bass as bass
import concourse.mybir as mybir
from concourse.bass_utils import run_bass_kernel_spmd

F32 = mybir.dt.float32
BF16 = mybir.dt.bfloat16
U8 = mybir.dt.uint8
AF = mybir.ActivationFunctionType
ALU = mybir.AluOpType
AX = mybir.AxisListType.X
MAGIC = 12582912.0
NCORES = 8
D = 1024
NH = 8
KC = 8
HC1 = 32
KB = 1024


class Buf:
    __slots__ = ("w", "r")

    def __init__(self, carry=()):
        self.w = None
        self.r = []
        for b in carry:
            if b.w is not None:
                self.r.append(b.w)
            self.r.extend(b.r)


class Prog:
    ENGS = ("tensor", "vector", "scalar", "gpsimd", "sync")
    NRING = 8

    def __init__(self, nc):
        self.nc = nc
        names = ["pe", "dve", "act", "pool", "cc"]
        names += [f"ds{i}" for i in range(self.NRING)]
        names += [f"dg{i}" for i in range(self.NRING)]
        self.sem = {n: nc.alloc_semaphore("s_" + n) for n in names}
        self.cnt = {n: 0 for n in names}
        self.items = {e: [] for e in self.ENGS}
        self.waited = {e: {n: 0 for n in names} for e in self.ENGS}
        self.rr = {"sync": 0, "gpsimd": 0}

    def emit(self, eng, fn, r=(), w=(), sem=None, extra=()):
        if sem is None:
            sem = {"tensor": ("pe", 1), "vector": ("dve", 1),
                   "scalar": ("act", 1), "gpsimd": ("pool", 1)}[eng]
        sname, amt = sem
        deps = {}

        def dep(tok):
            if tok is not None:
                s, v = tok
                if deps.get(s, 0) < v:
                    deps[s] = v
        for b in r:
            dep(b.w)
        for b in w:
            dep(b.w)
            for t in b.r:
                dep(t)
        for t in extra:
            dep(t)
        own = {"tensor": "pe", "vector": "dve", "scalar": "act",
               "gpsimd": "pool"}.get(eng)
        waits = []
        for s, v in deps.items():
            if s == own or self.waited[eng][s] < v:
                waits.append((s, v))
                if self.waited[eng][s] < v:
                    self.waited[eng][s] = v
        self.cnt[sname] += amt
        tok = (sname, self.cnt[sname])
        self.items[eng].append((waits, fn, self.sem[sname], amt))
        for b in r:
            b.r.append(tok)
        for b in w:
            b.w = tok
            b.r = []
        return tok

    def dma(self, queue, fn, r=(), w=(), extra=()):
        pfx = "ds" if queue == "sync" else "dg"
        slot = self.rr[queue] % self.NRING
        self.rr[queue] += 1
        sname = f"{pfx}{slot}"
        gate = (sname, self.cnt[sname]) if self.cnt[sname] else None
        extra = list(extra) + ([gate] if gate else [])
        return self.emit(queue, fn, r=r, w=w, sem=(sname, 16), extra=extra)

    def finish(self):
        finals = dict(self.cnt)
        with self.nc.Block() as block:
            for eng_name in self.ENGS:
                def make(engname):
                    def body(eng):
                        for waits, fn, semh, amt in self.items[engname]:
                            for s, v in waits:
                                eng.wait_ge(self.sem[s], v)
                            inst = fn(eng)
                            inst.then_inc(semh, amt)
                        for s, v in finals.items():
                            if v > self.waited[engname][s]:
                                eng.wait_ge(self.sem[s], v)
                    return body
                getattr(block, eng_name)(make(eng_name))


class Region:
    def __init__(self, handle, nbytes):
        self.h = handle
        self.nbytes = nbytes
        self.carves = []

    def carve(self, start, nbytes, dtype, ncols=None):
        assert start % 32 == 0 and start + nbytes <= self.nbytes, (start, nbytes)
        ap = self.h.ap()[:, start:start + nbytes]
        if dtype != U8:
            ap = ap.bitcast(dtype)
        if ncols is not None:
            ap = ap[:, 0:ncols]
        carry = [b for s, e, b in self.carves if s < start + nbytes and start < e]
        buf = Buf(carry)
        self.carves.append((start, start + nbytes, buf))
        return ap, buf

    def carve_n(self, start, n, step, dtype, ncols=None):
        aps, bufs = [], []
        for i in range(n):
            ap, b = self.carve(start + i * step, step, dtype, ncols)
            aps.append(ap)
            bufs.append(b)
        return aps, bufs


def build(T=2048):
    S = min(512, T)
    NSL = T // S
    TCS = S // 128
    NTC = T // 128
    THW = min(1024, T)
    NTH2 = T // THW
    TQW = 256
    assert T % S == 0 and S % 256 == 0

    nc = bass.Bass("TRN2", target_bir_lowering=False)

    xT = nc.dram_tensor("xT", [D, T], F32, kind="ExternalInput")
    wqkvT_d = nc.dram_tensor("wqkvT", [D, 3 * D], F32, kind="ExternalInput")
    woutT_d = nc.dram_tensor("woutT", [D, D], F32, kind="ExternalInput")
    projE_d = nc.dram_tensor("projE", [128, NH * 128], F32, kind="ExternalInput")
    boutpp_d = nc.dram_tensor("boutpp", [128, 8], F32, kind="ExternalInput")
    b1pp_d = nc.dram_tensor("b1pp", [128, 32], F32, kind="ExternalInput")
    b2row_d = nc.dram_tensor("b2row", [1, D], F32, kind="ExternalInput")
    w1blk_d = nc.dram_tensor("w1blk", [HC1, KC, 128, 128], F32, kind="ExternalInput")
    w2T_d = nc.dram_tensor("w2T", [4096, D], F32, kind="ExternalInput")
    idf_d = nc.dram_tensor("idf", [128, 128], F32, kind="ExternalInput")
    idb_d = nc.dram_tensor("idb", [128, 128], BF16, kind="ExternalInput")
    onesr_d = nc.dram_tensor("onesr", [1, 128], F32, kind="ExternalInput")
    onesb_d = nc.dram_tensor("onesb", [1, 128], BF16, kind="ExternalInput")
    invd_d = nc.dram_tensor("invd", [128, 1], BF16, kind="ExternalInput")
    cvec_d = nc.dram_tensor("cvec", [128, 4], F32, kind="ExternalInput")
    out_d = nc.dram_tensor("out", [T, D], F32, kind="ExternalOutput")

    hsp_d = nc.dram_tensor("hsp", [NTC, HC1, 128, 128], F32)
    hspb = [Buf() for _ in range(NTC)]
    mr_d = nc.dram_tensor("mrsp", [NSL, 2 * S], F32)
    mrb = [Buf() for _ in range(NSL)]
    cc1i_d = nc.dram_tensor("cc1i", [128, 1], F32)
    cc1o_d = nc.dram_tensor("cc1o", [128 * NCORES, 1], F32)
    cc2i_d = nc.dram_tensor("cc2i", [128, 1], F32)
    cc2o_d = nc.dram_tensor("cc2o", [128 * NCORES, 1], F32)
    cc1ib, cc1ob, cc2ib, cc2ob = Buf(), Buf(), Buf(), Buf()
    RG = [list(range(NCORES))]

    def sbt(name, shape, dt):
        return nc.alloc_sbuf_tensor(name, shape, dt)

    X2 = [sbt(f"x2_{k}", [128, T], F32) for k in range(KC)]
    X2B = [[Buf() for _ in range(NSL)] for _ in range(KC)]
    VECS = sbt("vecs", [128, 2 * S], F32)
    vmb, vrb = Buf(), Buf()
    SCAL = sbt("scal", [128, 64], F32)
    (SC_AMX, SC_TX, SC_SX, SC_SW1, SC_S1X, SC_AMH, SC_TH, SC_SH, SC_SW2,
     SC_S2X, SC_W1M, SC_W2M, SC_T1, SC_T2, SC_INVX, SC_INVW1, SC_INVH,
     SC_INVW2) = range(18)
    scb = [Buf() for _ in range(20)]
    BCV = sbt("bcv", [128, 16], F32)
    (BC_INVX, BC_S1X, BC_INVW1, BC_INVH, BC_S2X, BC_INVW2, BC_LMAX, BC_W1M,
     BC_W2M, BC_HMAX, BC_Z0, BC_Z1) = range(12)
    bcb = [Buf() for _ in range(16)]
    POOLT = [sbt(f"poolt{i}", [128, 1024], F32) for i in range(2)]
    ptb = [Buf(), Buf()]
    pt1h = [Buf(), Buf()]
    EXPT = [sbt(f"expt{i}", [128, 512], F32) for i in range(2)]
    exb = [Buf(), Buf()]
    STSB = [sbt(f"stsb{i}", [128, 64], BF16) for i in range(2)]
    stb = [Buf() for _ in range(2)]
    ATOK = [sbt(f"atok{i}", [128, 128], F32) for i in range(4)]
    atb = [Buf() for _ in range(4)]
    ZV = sbt("zv", [128, 8], F32)
    zvb = [Buf(), Buf()]
    IDF = sbt("idf_s", [128, 128], F32)
    IDB = sbt("idb_s", [128, 128], BF16)
    ONESR = sbt("onesr_s", [1, 128], F32)
    ONESB = sbt("onesb_s", [1, 128], BF16)
    INVD = sbt("invd_s", [128, 1], BF16)
    B1PP = sbt("b1pp_s", [128, 32], F32)
    B1S = sbt("b1s_s", [128, 32], F32)
    BOUT = sbt("bout_s", [128, 8], F32)
    B2R = sbt("b2r_s", [1, D], BF16)
    CV = sbt("cv_s", [128, 4], F32)
    cb = {n: Buf() for n in ("idf", "idb", "onesr", "onesb", "invd", "b1pp",
                             "b1s", "bout", "b2r", "proj", "cv")}

    AR_BYTES = 120 * KB
    ar_h = sbt("arena", [128, AR_BYTES], U8)
    AR = Region(ar_h, AR_BYTES)

    WQKV, wqkvb = AR.carve_n(0, KC, 6 * KB, BF16)
    WOUTS, woutb = AR.carve_n(48 * KB, KC, 2 * KB, BF16)
    LN1S, ln1b = AR.carve_n(64 * KB, KC, 1 * KB, BF16, ncols=S)
    QF, qfb = AR.carve_n(72 * KB, 2, 1 * KB, BF16, ncols=S)
    KF, kfb = AR.carve_n(74 * KB, 2, 1 * KB, BF16, ncols=S)
    QP, qpb = AR.carve_n(76 * KB, 2, 1 * KB, BF16, ncols=S)
    KP, kpb = AR.carve_n(78 * KB, 2, 1 * KB, BF16, ncols=S)
    AFM, afb = AR.carve_n(80 * KB, KC, 1 * KB, BF16, ncols=S)
    VT, vtb = AR.carve_n(88 * KB, TCS, 2080, BF16, ncols=1032)
    XB16, xbb = AR.carve_n(97 * KB, KC, 1 * KB, BF16, ncols=S)
    SCANB, scnb = AR.carve_n(105 * KB, 2, 2 * KB, F32)
    MBR, mbrb = AR.carve_n(109 * KB, 2, 2 * KB, F32, ncols=S)
    SQB, sqb = AR.carve_n(113 * KB, 4, 512, BF16, ncols=TQW)
    PROJS_l, projb_l = AR.carve_n(115 * KB, 1, 2 * KB, BF16)
    PROJS = PROJS_l[0]
    cb["proj"] = projb_l[0]

    PBW = [nc.alloc_psum_tensor(f"pbw{i}", [128, 1024], F32) for i in range(2)]
    pwb = [[Buf(), Buf()], [Buf(), Buf()]]
    PM = [nc.alloc_psum_tensor(f"pm{i}", [128, 512], F32) for i in range(2)]
    pmb = [Buf(), Buf()]
    CB = [nc.alloc_psum_tensor(f"cb{i}", [128, 512], F32) for i in range(2)]
    cbb = [Buf(), Buf()]
    CB_ST = [CB[i][:, 0:64] for i in range(2)]
    CB_NUM = [[CB[i][:, 64:193], CB[i][:, 200:329]] for i in range(2)]
    CB_DEN2 = [CB[i].rearrange("p (a b) -> p a b", b=1)[:, :, 0]
               for i in range(2)]
    CB_TR = [CB[i][:, 336:464] for i in range(2)]

    pr = Prog(nc)
    E = pr.emit
    DMA = pr.dma

    DMA("sync", lambda e: e.dma_start(out=IDF[:], in_=idf_d[:]), w=[cb["idf"]])
    DMA("sync", lambda e: e.dma_start(out=IDB[:], in_=idb_d[:]), w=[cb["idb"]])
    DMA("sync", lambda e: e.dma_start(out=ONESR[:], in_=onesr_d[:]), w=[cb["onesr"]])
    DMA("sync", lambda e: e.dma_start(out=ONESB[:], in_=onesb_d[:]), w=[cb["onesb"]])
    DMA("sync", lambda e: e.dma_start(out=INVD[:], in_=invd_d[:]), w=[cb["invd"]])
    DMA("sync", lambda e: e.dma_start(out=B1PP[:], in_=b1pp_d[:]), w=[cb["b1pp"]])
    DMA("sync", lambda e: e.dma_start(out=BOUT[:], in_=boutpp_d[:]), w=[cb["bout"]])
    DMA("sync", lambda e: e.dma_start(out=CV[:], in_=cvec_d[:]), w=[cb["cv"]])
    DMA("gpsimd", lambda e: e.dma_start(out=B2R[:], in_=b2row_d[:]), w=[cb["b2r"]])
    DMA("gpsimd", lambda e: e.dma_start(out=PROJS[:], in_=projE_d[:]), w=[cb["proj"]])
    for k in range(KC):
        DMA("sync", lambda e, k=k: e.dma_start(
            out=X2[k][:, 0:S], in_=xT[k * 128:(k + 1) * 128, 0:S]),
            w=[X2B[k][0]])
    vcast_copies = []
    if T >= 2048:
        for k in range(KC):
            DMA("sync", lambda e, k=k: e.dma_start(
                out=X2[k][:, 512:1536],
                in_=wqkvT_d[k * 128:(k + 1) * 128, 2048:3072]),
                w=[X2B[k][1], X2B[k][2]])

            def _cp(k=k):
                E("scalar", lambda e, k=k: e.copy(
                    WQKV[k][:, 2048:3072], X2[k][:, 512:1536]),
                  r=[X2B[k][1], X2B[k][2]], w=[wqkvb[k]])
            vcast_copies.append(_cp)
        parts = ((0, 1024), (1024, 2048))
    else:
        parts = ((2048, 3072), (0, 1024), (1024, 2048))
    for part0, part1 in parts:
        for k in range(KC):
            DMA("gpsimd", lambda e, k=k, part0=part0, part1=part1: e.dma_start(
                out=WQKV[k][:, part0:part1],
                in_=wqkvT_d[k * 128:(k + 1) * 128, part0:part1]),
                w=[wqkvb[k]])
    for k in range(KC):
        DMA("gpsimd", lambda e, k=k: e.dma_start(
            out=WOUTS[k][:], in_=woutT_d[k * 128:(k + 1) * 128, :]), w=[woutb[k]])
    for tc in range(TCS):
        E("vector", lambda e, tc=tc: e.memset(VT[tc][:], 1.0), w=[vtb[tc]])
    for c in (BC_W1M, BC_W2M, BC_LMAX, BC_HMAX):
        E("vector", lambda e, c=c: e.memset(BCV[:, c:c + 1], 0.0), w=[bcb[c]])

    w1_tiles = [(hc, hf) for hc in range(HC1) for hf in range(2)]
    sc_dma = [0]
    sc_red = [0]

    def scan_dmas(n):
        for _ in range(n):
            j = sc_dma[0]
            if j >= len(w1_tiles):
                return
            sc_dma[0] += 1
            hc, hf = w1_tiles[j]
            i = j % 2
            src_ap = w1blk_d[hc, hf * 4:(hf + 1) * 4].rearrange("kc p h -> p kc h")
            DMA("gpsimd", lambda e, src_ap=src_ap, i=i: e.dma_start(
                out=SCANB[i].rearrange("p (kc h) -> p kc h", h=128), in_=src_ap),
                w=[scnb[i]])

    def scan_reds(n):
        for _ in range(n):
            if sc_red[0] >= sc_dma[0] or sc_red[0] >= len(w1_tiles):
                return
            j = sc_red[0]
            sc_red[0] += 1
            i = j % 2
            E("vector", lambda e, i=i: e.tensor_reduce(
                BCV[:, BC_Z0:BC_Z0 + 1], SCANB[i][:], AX, ALU.max,
                apply_absolute_value=True),
              r=[scnb[i]], w=[bcb[BC_Z0]])
            E("vector", lambda e: e.tensor_tensor(
                BCV[:, BC_W1M:BC_W1M + 1], BCV[:, BC_W1M:BC_W1M + 1],
                BCV[:, BC_Z0:BC_Z0 + 1], ALU.max),
              r=[bcb[BC_Z0]], w=[bcb[BC_W1M]])

    def scan_step():
        if sc_red[0] + 2 <= sc_dma[0]:
            scan_reds(2)
        scan_dmas(2)

    def ln_stats(src_aps, src_bufs):
        for tq in range(S // TQW):
            E("tensor", lambda e, tq=tq: [
                e.matmul(PM[0][0:1, 0:TQW], INVD[:],
                         src_aps[k][:, tq * TQW:(tq + 1) * TQW],
                         start=(k == 0), stop=(k == KC - 1))
                for k in range(KC)][-1],
              r=[cb["invd"]] + list(src_bufs), w=[pmb[0]])
            for k in range(KC):
                i = k % 4
                E("vector", lambda e, k=k, i=i, tq=tq: e.tensor_tensor(
                    SQB[i][:], src_aps[k][:, tq * TQW:(tq + 1) * TQW],
                    src_aps[k][:, tq * TQW:(tq + 1) * TQW], ALU.mult),
                  r=[src_bufs[k]], w=[sqb[i]])
                if k % 4 == 3:
                    E("tensor", lambda e, k=k: [
                        e.matmul(PM[0][0:1, TQW:2 * TQW], INVD[:], SQB[j][:],
                                 start=(k - 3 + j == 0),
                                 stop=(k - 3 + j == KC - 1))
                        for j in range(4)][-1],
                      r=[cb["invd"]] + [sqb[j] for j in range(4)], w=[pmb[0]])
            c0 = tq * TQW
            E("vector", lambda e, c0=c0: e.tensor_copy(
                VECS[0:1, c0:c0 + TQW], PM[0][0:1, 0:TQW]),
              r=[pmb[0]], w=[vmb])
            E("vector", lambda e, c0=c0: e.tensor_tensor(
                POOLT[0][0:1, 0:TQW], VECS[0:1, c0:c0 + TQW],
                VECS[0:1, c0:c0 + TQW], ALU.mult),
              r=[vmb], w=[ptb[0]])
            E("vector", lambda e: e.tensor_tensor(
                POOLT[0][0:1, 256:256 + TQW], PM[0][0:1, TQW:2 * TQW],
                POOLT[0][0:1, 0:TQW], ALU.subtract),
              r=[pmb[0], ptb[0]], w=[ptb[0]])
            E("scalar", lambda e: e.activation(
                POOLT[0][0:1, 0:TQW], POOLT[0][0:1, 256:256 + TQW], AF.Ln,
                bias=CV[0:1, 1:2]),
              r=[ptb[0], cb["cv"]], w=[ptb[0]])
            E("scalar", lambda e, c0=c0: e.activation(
                VECS[0:1, S + c0:S + c0 + TQW], POOLT[0][0:1, 0:TQW], AF.Exp,
                scale=CV[0:1, 2:3]),
              r=[ptb[0], cb["cv"]], w=[vrb])

    def bcast_mr(eng="vector"):
        def cp(dst, srcap):
            if eng == "vector":
                E("vector", lambda e, dst=dst, srcap=srcap: e.tensor_copy(
                    dst[0][:], srcap), r=[pmb[1]], w=[dst[1]])
            else:
                E("scalar", lambda e, dst=dst, srcap=srcap: e.copy(
                    dst[0][:], srcap), r=[pmb[1]], w=[dst[1]])
        E("tensor", lambda e: e.matmul(
            PM[1][:, 0:S], ONESR[:], VECS[0:1, 0:S], start=True, stop=True),
          r=[cb["onesr"], vmb], w=[pmb[1]])
        cp((MBR[0], mbrb[0]), PM[1][:, 0:S])
        E("tensor", lambda e: e.matmul(
            PM[1][:, 0:S], ONESR[:], VECS[0:1, S:2 * S], start=True, stop=True),
          r=[cb["onesr"], vrb], w=[pmb[1]])
        cp((MBR[1], mbrb[1]), PM[1][:, 0:S])

    def bcast_scal(col, sbuf_, bc):
        E("tensor", lambda e, col=col: e.matmul(
            PM[1][:, 0:1], ONESR[:], SCAL[0:1, col:col + 1],
            start=True, stop=True),
          r=[cb["onesr"], sbuf_], w=[pmb[1]])
        E("vector", lambda e, bc=bc: e.tensor_copy(
            BCV[:, bc:bc + 1], PM[1][:, 0:1]),
          r=[pmb[1]], w=[bcb[bc]])

    slot_ctr = [0]

    def next_slot():
        n = slot_ctr[0] % 4
        slot_ctr[0] += 1
        i, j = n // 2, n % 2
        return PBW[i][:, j * 512:(j + 1) * 512], pwb[i][j]

    def x2_load(slc):
        t0 = slc * S
        for k in range(KC):
            DMA("sync", lambda e, k=k, t0=t0: e.dma_start(
                out=X2[k][:, t0:t0 + S],
                in_=xT[k * 128:(k + 1) * 128, t0:t0 + S]),
                w=[X2B[k][slc]])

    def stage_ln_stats(slc):
        t0 = slc * S
        for k in range(KC):
            E("scalar", lambda e, k=k, t0=t0: e.copy(
                XB16[k][:], X2[k][:, t0:t0 + S]),
              r=[X2B[k][slc]], w=[xbb[k]])
        ln_stats([XB16[k] for k in range(KC)], xbb)
        bcast_mr()

    def stage_ln_norm(slc):
        t0 = slc * S
        for k in range(KC):
            hh = k % 2
            E("gpsimd", lambda e, k=k, t0=t0, hh=hh: e.tensor_tensor(
                POOLT[1][:, hh * 512:hh * 512 + S], X2[k][:, t0:t0 + S],
                MBR[0][:], ALU.subtract),
              r=[X2B[k][slc], mbrb[0]], w=[pt1h[hh]])
            E("vector", lambda e, k=k, hh=hh: e.tensor_tensor(
                LN1S[k][:], POOLT[1][:, hh * 512:hh * 512 + S], MBR[1][:],
                ALU.mult),
              r=[pt1h[hh], mbrb[1]], w=[ln1b[k]])

    stage_ln_stats(0)
    stage_ln_norm(0)
    for cp in vcast_copies:
        cp()
    for slc in range(NSL):
        t0 = slc * S
        if slc + 1 < NSL:
            x2_load(slc + 1)
        for tc in range(TCS):
            for half in range(2):
                ps, psb = next_slot()
                if tc == 0:
                    for k in range(KC):
                        E("tensor", lambda e, tc=tc, half=half, ps=ps, k=k:
                            e.matmul(ps, LN1S[k][:, tc * 128:(tc + 1) * 128],
                                     WQKV[k][:, 2048 + half * 512:
                                              2048 + (half + 1) * 512],
                                     start=(k == 0), stop=(k == KC - 1)),
                          r=[ln1b[k], wqkvb[k]], w=[psb])
                else:
                    E("tensor", lambda e, tc=tc, half=half, ps=ps: [
                        e.matmul(ps, LN1S[k][:, tc * 128:(tc + 1) * 128],
                                 WQKV[k][:, 2048 + half * 512:
                                          2048 + (half + 1) * 512],
                                 start=(k == 0), stop=(k == KC - 1))
                        for k in range(KC)][-1],
                      r=[ln1b[kk] for kk in range(KC)]
                        + [wqkvb[kk] for kk in range(KC)],
                      w=[psb])
                E("vector", lambda e, tc=tc, half=half, ps=ps: e.tensor_copy(
                    VT[tc].rearrange("p (h c) -> p h c", c=129)
                        [:, 4 * half:4 * half + 4, 0:128],
                    ps.rearrange("p (h t) -> p h t", t=128)),
                  r=[psb], w=[vtb[tc]])

        def qk_group(wq_oc, h, dst_aps, dst_bufs):
            hb = h % 2
            ps, psb = next_slot()
            E("tensor", lambda e, wq_oc=wq_oc, ps=ps: [
                e.matmul(ps[:, 0:S], WQKV[k][:, wq_oc * 128:(wq_oc + 1) * 128],
                         LN1S[k][:], start=(k == 0), stop=(k == KC - 1))
                for k in range(KC)][-1],
              r=[ln1b[kk] for kk in range(KC)] + [wqkvb[kk] for kk in range(KC)],
              w=[psb])
            E("scalar", lambda e, hb=hb, ps=ps: e.copy(
                dst_aps[hb][:], ps[:, 0:S]),
              r=[psb], w=[dst_bufs[hb]])

        def proj_elu(h, dst_aps, dst_bufs, pdst_aps, pdst_bufs, eb):
            hb = h % 2
            ps2, psb2 = next_slot()
            E("tensor", lambda e, h=h, hb=hb, ps2=ps2: e.matmul(
                ps2[:, 0:S], PROJS[:, h * 128:(h + 1) * 128], dst_aps[hb][:],
                start=True, stop=True),
              r=[cb["proj"], dst_bufs[hb]], w=[psb2])
            E("scalar", lambda e, ps2=ps2, eb=eb: e.activation(
                EXPT[eb][:, 0:S], ps2[:, 0:S], AF.Exp),
              r=[psb2], w=[exb[eb]])
            E("vector", lambda e, eb=eb: e.tensor_scalar(
                EXPT[eb][:, 0:S], EXPT[eb][:, 0:S], 1.0, None, ALU.min),
              r=[exb[eb]], w=[exb[eb]])
            E("vector", lambda e, hb=hb, ps2=ps2, eb=eb: e.scalar_tensor_tensor(
                pdst_aps[hb][:], ps2[:, 0:S], 0.0, EXPT[eb][:, 0:S],
                ALU.max, ALU.add),
              r=[psb2, exb[eb]], w=[pdst_bufs[hb]])

        def chains_part1_st(h):
            hb = h % 2
            for b in range(TCS // 2):
                E("tensor", lambda e, h=h, b=b, hb=hb: [
                    e.matmul(CB_ST[b][32 * c:32 * (c + 1), 32 * t:32 * (t + 1)],
                             KP[hb][:, (2 * b + t) * 128 + 32 * c:
                                      (2 * b + t) * 128 + 32 * (c + 1)],
                             QP[hb][:, (2 * b + t) * 128 + 32 * c:
                                      (2 * b + t) * 128 + 32 * (c + 1)],
                             start=True, stop=True, tile_position=(0, 32 * c))
                    for t in range(2) for c in range(4)][-1],
                  r=[kpb[hb], qpb[hb]], w=[cbb[b]])
                E("scalar", lambda e, b=b: e.copy(STSB[b][:], CB_ST[b]),
                  r=[cbb[b]], w=[stb[b]])

        def chains_part1_num(h):
            for b in range(TCS // 2):
                E("tensor", lambda e, h=h, b=b: [
                    e.matmul(CB_NUM[b][t][32 * c:32 * (c + 1), :],
                             STSB[b][32 * c:32 * (c + 1), 32 * t:32 * (t + 1)],
                             VT[2 * b + t][32 * c:32 * (c + 1),
                                           h * 129:(h + 1) * 129],
                             start=True, stop=True, tile_position=(32 * c, 32 * c))
                    for t in range(2) for c in range(4)][-1],
                  r=[stb[b], vtb[2 * b], vtb[2 * b + 1]], w=[cbb[b]])

        def chains_part1(h):
            chains_part1_st(h)
            chains_part1_num(h)

        def chains_part2z(h):
            for b in range(TCS // 2):
                for t in range(2):
                    E("vector", lambda e, b=b, t=t: e.tensor_scalar(
                        ZV[:, 4 * b + t:4 * b + t + 1],
                        CB_NUM[b][t][:, 128:129], 1e-8, None, ALU.add),
                      r=[cbb[b]], w=[zvb[b]])
                E("vector", lambda e, b=b: e.reciprocal(
                    ZV[:, 4 * b + 2:4 * b + 4], ZV[:, 4 * b:4 * b + 2]),
                  r=[zvb[b]], w=[zvb[b]])
                for t in range(2):
                    E("vector", lambda e, b=b, t=t: e.tensor_scalar(
                        ATOK[2 * b + t][:], CB_NUM[b][t][:, 0:128],
                        ZV[:, 4 * b + 2 + t:4 * b + 3 + t], None, ALU.mult),
                      r=[cbb[b], zvb[b]], w=[atb[2 * b + t]])

        def chains_part2t(h):
            for t in range(2):
                for b in range(TCS // 2):
                    E("tensor", lambda e, b=b, t=t: e.transpose(
                        CB_TR[b], ATOK[2 * b + t][:], IDF[:]),
                      r=[atb[2 * b + t], cb["idf"]], w=[cbb[b]])
                    E("scalar", lambda e, h=h, b=b, t=t: e.copy(
                        AFM[h][:, (2 * b + t) * 128:(2 * b + t + 1) * 128],
                        CB_TR[b]),
                      r=[cbb[b]], w=[afb[h]])

        pend = None
        pend_t = None
        for h in range(NH):
            qk_group(h, h, QF, qfb)
            if pend_t is not None:
                chains_part2t(pend_t)
                pend_t = None
            if pend is not None:
                chains_part1_st(pend)
            qk_group(NH + h, h, KF, kfb)
            if pend is not None:
                chains_part1_num(pend)
                chains_part2z(pend)
            proj_elu(h, QF, qfb, QP, qpb, 0)
            proj_elu(h, KF, kfb, KP, kpb, 1)
            if pend is not None:
                pend_t = pend
            pend = h
            scan_step()
        if pend_t is not None:
            chains_part2t(pend_t)
        chains_part1(pend)
        chains_part2z(pend)
        chains_part2t(pend)

        for oc in range(KC):
            ps, psb = next_slot()
            E("tensor", lambda e, oc=oc, ps=ps: [
                e.matmul(ps[:, 0:S], WOUTS[k][:, oc * 128:(oc + 1) * 128],
                         AFM[k][:], start=(k == 0), stop=(k == KC - 1))
                for k in range(KC)][-1],
              r=[afb[kk] for kk in range(KC)] + [woutb[kk] for kk in range(KC)],
              w=[psb])
            E("vector", lambda e, oc=oc, ps=ps, t0=t0: e.scalar_tensor_tensor(
                X2[oc][:, t0:t0 + S], ps[:, 0:S], BOUT[:, oc:oc + 1],
                X2[oc][:, t0:t0 + S], ALU.add, ALU.add),
              r=[psb, cb["bout"], X2B[oc][slc]], w=[X2B[oc][slc]])

        if slc + 1 < NSL:
            stage_ln_stats(slc + 1)
            stage_ln_norm(slc + 1)

        for k in range(KC):
            E("scalar", lambda e, k=k, t0=t0: e.copy(
                XB16[k][:], X2[k][:, t0:t0 + S]),
              r=[X2B[k][slc]], w=[xbb[k]])
        ln_stats([XB16[k] for k in range(KC)], xbb)
        DMA("sync", lambda e, slc=slc: e.dma_start(
            out=mr_d[slc:slc + 1, :], in_=VECS[0:1, 0:2 * S]),
            r=[vmb, vrb], w=[mrb[slc]])

        bcast_mr()
        for k in range(KC):
            hh = k % 2
            E("gpsimd", lambda e, k=k, t0=t0, hh=hh: e.tensor_tensor(
                POOLT[1][:, hh * 512:hh * 512 + S], X2[k][:, t0:t0 + S],
                MBR[0][:], ALU.subtract),
              r=[X2B[k][slc], mbrb[0]], w=[pt1h[hh]])
            E("vector", lambda e, hh=hh: e.tensor_tensor(
                POOLT[0][:, hh * 512:hh * 512 + S],
                POOLT[1][:, hh * 512:hh * 512 + S], MBR[1][:], ALU.mult),
              r=[pt1h[hh], mbrb[1]], w=[ptb[0]])
            E("vector", lambda e, hh=hh: e.tensor_reduce(
                BCV[:, BC_Z1:BC_Z1 + 1], POOLT[0][:, hh * 512:hh * 512 + S],
                AX, ALU.max, apply_absolute_value=True),
              r=[ptb[0]], w=[bcb[BC_Z1]])
            E("vector", lambda e: e.tensor_tensor(
                BCV[:, BC_LMAX:BC_LMAX + 1], BCV[:, BC_LMAX:BC_LMAX + 1],
                BCV[:, BC_Z1:BC_Z1 + 1], ALU.max),
              r=[bcb[BC_Z1]], w=[bcb[BC_LMAX]])

    DMA("gpsimd", lambda e: e.dma_start(
        out=cc1i_d[:], in_=BCV[:, BC_LMAX:BC_LMAX + 1]),
        r=[bcb[BC_LMAX]], w=[cc1ib])
    E("gpsimd", lambda e: e.collective_compute(
        "AllGather", ALU.bypass, replica_groups=RG,
        ins=[cc1i_d.ap().opt()], outs=[cc1o_d.ap().opt()]),
      r=[cc1ib], w=[cc1ob], sem=("cc", 1))

    XQ, xqb = AR.carve_n(64 * KB, KC, T * 2, BF16)
    xqsb = [[Buf() for _ in range(NSL)] for _ in range(KC)]

    def p2_stats(slc):
        DMA("sync", lambda e, slc=slc: e.dma_start(
            out=VECS[0:1, 0:2 * S], in_=mr_d[slc:slc + 1, :]),
            r=[mrb[slc]], w=[vmb, vrb])

    def part_reduce(col, dst):
        E("tensor", lambda e, col=col: e.transpose(
            PM[1][0:1, 0:128], BCV[:, col:col + 1], IDF[:]),
          r=[bcb[col], cb["idf"]], w=[pmb[1]])
        E("vector", lambda e, dst=dst: e.tensor_reduce(
            SCAL[0:1, dst:dst + 1], PM[1][0:1, 0:128], AX, ALU.max),
          r=[pmb[1]], w=[scb[dst]])

    def scale_chain(src_col, t_slot, s_slot, inv_slot):
        E("vector", lambda e: e.tensor_scalar(
            SCAL[0:1, t_slot:t_slot + 1], SCAL[0:1, src_col:src_col + 1],
            1e-6, None, ALU.max),
          r=[scb[src_col]], w=[scb[t_slot]])
        E("vector", lambda e: e.tensor_scalar(
            SCAL[0:1, s_slot:s_slot + 1], SCAL[0:1, t_slot:t_slot + 1],
            1.0 / 127.0, None, ALU.mult),
          r=[scb[t_slot]], w=[scb[s_slot]])
        E("vector", lambda e: e.reciprocal(
            SCAL[0:1, SC_T1:SC_T1 + 1], SCAL[0:1, t_slot:t_slot + 1]),
          r=[scb[t_slot]], w=[scb[SC_T1]])
        E("vector", lambda e: e.tensor_scalar(
            SCAL[0:1, inv_slot:inv_slot + 1], SCAL[0:1, SC_T1:SC_T1 + 1],
            127.0, None, ALU.mult),
          r=[scb[SC_T1]], w=[scb[inv_slot]])

    p2_stats(0)
    scan_reds(len(w1_tiles))
    part_reduce(BC_W1M, SC_W1M)
    scale_chain(SC_W1M, SC_T2, SC_SW1, SC_INVW1)
    bcast_scal(SC_INVW1, scb[SC_INVW1], BC_INVW1)

    WSP, wspb = AR.carve_n(96 * KB, 4, 2 * KB, F32)
    HEV, hevb = AR.carve_n(104 * KB, 4, 2 * KB, F32)
    W1Q, w1qb = AR.carve_n(114 * KB, 3, 2 * KB, BF16)
    wsp_ctr = [0]

    w2_pos = [0]
    hmax_pend = []
    w2scan_pos = [0]

    def emit_w2_scans(n):
        for _ in range(n):
            j = w2scan_pos[0]
            if j >= 64:
                return
            w2scan_pos[0] += 1
            i = wsp_ctr[0] % 4
            wsp_ctr[0] += 1
            DMA("sync" if j % 2 == 0 else "gpsimd",
                lambda e, j=j, i=i: e.dma_start(
                    out=WSP[i][:],
                    in_=w2T_d[j * 64:(j + 1) * 64, :].rearrange(
                        "a (b d) -> (a b) d", b=2)),
                w=[wspb[i]])
            E("vector", lambda e, i=i: e.tensor_reduce(
                BCV[:, BC_Z0:BC_Z0 + 1], WSP[i][:], AX, ALU.max,
                apply_absolute_value=True),
              r=[wspb[i]], w=[bcb[BC_Z0]])
            E("vector", lambda e: e.tensor_tensor(
                BCV[:, BC_W2M:BC_W2M + 1], BCV[:, BC_W2M:BC_W2M + 1],
                BCV[:, BC_Z0:BC_Z0 + 1], ALU.max),
              r=[bcb[BC_Z0]], w=[bcb[BC_W2M]])

    def emit_w2_round(n):
        for _ in range(n):
            if w2_pos[0] >= HC1:
                return
            j = w2_pos[0]
            w2_pos[0] += 1
            for hf in range(2):
                i = wsp_ctr[0] % 4
                wsp_ctr[0] += 1
                DMA("sync", lambda e, j=j, hf=hf, i=i: e.dma_start(
                    out=WSP[i][:], in_=w2T_d[j * 128:(j + 1) * 128,
                                             hf * 512:(hf + 1) * 512]),
                    w=[wspb[i]])
                E("scalar", lambda e, i=i: e.activation(
                    WSP[i][:], WSP[i][:], AF.Copy, bias=MAGIC,
                    scale=BCV[:, BC_INVW2:BC_INVW2 + 1]),
                  r=[wspb[i], bcb[BC_INVW2]], w=[wspb[i]])
                E("vector", lambda e, j=j, hf=hf, i=i: e.tensor_scalar(
                    W2Q[j][:, hf * 512:(hf + 1) * 512], WSP[i][:],
                    MAGIC, None, ALU.subtract),
                  r=[wspb[i]], w=[w2qb[j]])

    def w1_round(hc):
        iw = hc % 3
        for hf in range(2):
            i2 = wsp_ctr[0] % 4
            wsp_ctr[0] += 1
            DMA("sync", lambda e, hc=hc, hf=hf, i2=i2: e.dma_start(
                out=WSP[i2].rearrange("p (kc h) -> p kc h", h=128),
                in_=w1blk_d[hc, hf * 4:(hf + 1) * 4].rearrange("kc p h -> p kc h")),
                w=[wspb[i2]])
            E("scalar", lambda e, i2=i2: e.activation(
                WSP[i2][:], WSP[i2][:], AF.Copy, bias=MAGIC,
                scale=BCV[:, BC_INVW1:BC_INVW1 + 1]),
              r=[wspb[i2], bcb[BC_INVW1]], w=[wspb[i2]])
            E("vector", lambda e, hf=hf, i2=i2, iw=iw: e.tensor_scalar(
                W1Q[iw][:, hf * 512:(hf + 1) * 512], WSP[i2][:],
                MAGIC, None, ALU.subtract),
              r=[wspb[i2]], w=[w1qb[iw]])

    w1_round(0)
    w1_round(1)
    w1_round(2)

    gland = Buf()

    def land_gather(out_d, dst_col):
        n = 128 * NCORES
        nr = (n + 2 * S - 1) // (2 * S)
        w = n // nr
        for r in range(nr):
            DMA("gpsimd", lambda e, r=r, w=w: e.dma_start(
                out=VECS[32 * (r + 1):32 * (r + 1) + 1, 0:w],
                in_=out_d[r * w:(r + 1) * w].rearrange(
                    "(o p) one -> o (p one)", o=1)),
                r=[cc1ob if out_d is cc1o_d else cc2ob], w=[gland])
        for r in range(nr):
            tcol = dst_col if r == 0 else SC_T1
            E("vector", lambda e, r=r, w=w, tcol=tcol: e.tensor_reduce(
                SCAL[0:1, tcol:tcol + 1],
                VECS[32 * (r + 1):32 * (r + 1) + 1, 0:w], AX,
                ALU.max, apply_absolute_value=True),
              r=[gland], w=[scb[tcol]])
            if r > 0:
                E("vector", lambda e, dst_col=dst_col: e.tensor_tensor(
                    SCAL[0:1, dst_col:dst_col + 1],
                    SCAL[0:1, dst_col:dst_col + 1],
                    SCAL[0:1, SC_T1:SC_T1 + 1], ALU.max),
                  r=[scb[SC_T1]], w=[scb[dst_col]])

    land_gather(cc1o_d, SC_AMX)
    scale_chain(SC_AMX, SC_TX, SC_SX, SC_INVX)
    bcast_scal(SC_INVX, scb[SC_INVX], BC_INVX)
    E("vector", lambda e: e.tensor_tensor(
        SCAL[0:1, SC_S1X:SC_S1X + 1], SCAL[0:1, SC_SW1:SC_SW1 + 1],
        SCAL[0:1, SC_SX:SC_SX + 1], ALU.mult),
      r=[scb[SC_SW1], scb[SC_SX]], w=[scb[SC_S1X]])
    bcast_scal(SC_S1X, scb[SC_S1X], BC_S1X)
    E("vector", lambda e: e.tensor_scalar(
        B1S[:], B1PP[:], BCV[:, BC_S1X:BC_S1X + 1], None, ALU.mult),
      r=[cb["b1pp"], bcb[BC_S1X]], w=[cb["b1s"]])

    pt0h = [Buf(carry=[ptb[0]]), Buf(carry=[ptb[0]])]

    def xqpass(slc):
        t0 = slc * S
        bcast_mr(eng="scalar")
        for k in range(KC):
            hh = k % 2
            E("gpsimd", lambda e, k=k, t0=t0, hh=hh: e.tensor_tensor(
                POOLT[1][:, hh * 512:hh * 512 + S], X2[k][:, t0:t0 + S],
                MBR[0][:], ALU.subtract),
              r=[X2B[k][slc], mbrb[0]], w=[pt1h[hh]])
            E("vector", lambda e, hh=hh: e.tensor_tensor(
                POOLT[0][:, hh * 512:hh * 512 + S],
                POOLT[1][:, hh * 512:hh * 512 + S], MBR[1][:], ALU.mult),
              r=[pt1h[hh], mbrb[1]], w=[pt0h[hh]])
            E("vector", lambda e, hh=hh: e.tensor_scalar(
                POOLT[0][:, hh * 512:hh * 512 + S],
                POOLT[0][:, hh * 512:hh * 512 + S],
                BCV[:, BC_INVX:BC_INVX + 1], MAGIC, ALU.mult, ALU.add),
              r=[bcb[BC_INVX]], w=[pt0h[hh]])
            E("scalar", lambda e, k=k, t0=t0, hh=hh: e.activation(
                XQ[k][:, t0:t0 + S], POOLT[0][:, hh * 512:hh * 512 + S],
                AF.Copy, bias=-MAGIC, scale=1.0),
              r=[pt0h[hh]], w=[xqb[k], xqsb[k][slc]])

    for slc in range(NSL):
        xqpass(slc)
        if slc + 1 < NSL:
            p2_stats(slc + 1)
    for b in pt0h:
        if b.w is not None:
            ptb[0].r.append(b.w)
        ptb[0].r.extend(b.r)
    emit_w2_scans(12)

    W2Q, w2qb = AR.carve_n(0, HC1, 2 * KB, BF16)

    for hc in range(HC1):
        iw = hc % 3
        for th in range(NTH2):
            pb_i = (hc * NTH2 + th) % 2
            W512 = min(512, THW)
            mm1_tok = E("tensor", lambda e, i=iw, th=th, pb_i=pb_i, W512=W512: [
                e.matmul(PBW[pb_i][:, half * W512:(half + 1) * W512],
                         W1Q[i][:, k * 128:(k + 1) * 128],
                         XQ[k][:, th * THW + half * W512:th * THW + (half + 1) * W512],
                         start=(k == 0), stop=(k == KC - 1))
                for k in range(KC) for half in range(THW // W512)][-1],
              r=[w1qb[iw]] + [xqsb[k][th * (THW // S) + j]
                              for k in range(KC) for j in range(THW // S)],
              w=[pwb[pb_i][0], pwb[pb_i][1]])
            for k in range(KC):
                xqb[k].r.append(mm1_tok)
            WH = min(512, THW)
            NHLF = THW // WH
            NTCH = WH // 128
            for hf2 in range(NHLF):
                hv = (NHLF * (hc * NTH2 + th) + hf2) % 4
                E("scalar", lambda e, hc=hc, th=th, hf2=hf2, hv=hv, pb_i=pb_i,
                    WH=WH: e.activation(
                        HEV[hv][:, 0:WH], PBW[pb_i][:, hf2 * WH:(hf2 + 1) * WH],
                        AF.Gelu, bias=B1S[:, hc:hc + 1],
                        scale=BCV[:, BC_S1X:BC_S1X + 1]),
                  r=[pwb[pb_i][hf2 * WH // 512], cb["b1s"], bcb[BC_S1X]],
                  w=[hevb[hv]])
                tc0 = th * (THW // 128) + hf2 * NTCH
                DMA("gpsimd", lambda e, hc=hc, tc0=tc0, hv=hv, NTCH=NTCH,
                    WH=WH: e.dma_start(
                    out=hsp_d[tc0:tc0 + NTCH, hc, :, :].rearrange("tc p t -> p tc t"),
                    in_=HEV[hv][:, 0:WH].rearrange("p (tc t) -> p tc t", t=128)),
                    r=[hevb[hv]], w=[hspb[t] for t in range(tc0, tc0 + NTCH)])
                hmax_pend.append(hv)
                if len(hmax_pend) > 2:
                    hv_o = hmax_pend.pop(0)
                    E("vector", lambda e, hv_o=hv_o, WH=WH: e.tensor_reduce(
                        BCV[:, BC_Z1:BC_Z1 + 1], HEV[hv_o][:, 0:WH], AX, ALU.max,
                        apply_absolute_value=True),
                      r=[hevb[hv_o]], w=[bcb[BC_Z1]])
                    E("vector", lambda e: e.tensor_tensor(
                        BCV[:, BC_HMAX:BC_HMAX + 1],
                        BCV[:, BC_HMAX:BC_HMAX + 1],
                        BCV[:, BC_Z1:BC_Z1 + 1], ALU.max),
                      r=[bcb[BC_Z1]], w=[bcb[BC_HMAX]])
        if hc + 3 < HC1:
            w1_round(hc + 3)
        if hc < 11:
            emit_w2_scans(6)
        else:
            if hc == 11:
                emit_w2_scans(64)
                part_reduce(BC_W2M, SC_W2M)
                scale_chain(SC_W2M, SC_T2, SC_SW2, SC_INVW2)
                bcast_scal(SC_INVW2, scb[SC_INVW2], BC_INVW2)
            emit_w2_round(2)
    WHF = min(512, THW)
    while hmax_pend:
        pb_o = hmax_pend.pop(0)
        E("vector", lambda e, pb_o=pb_o: e.tensor_reduce(
            BCV[:, BC_Z1:BC_Z1 + 1], HEV[pb_o][:, 0:WHF], AX, ALU.max,
            apply_absolute_value=True),
          r=[hevb[pb_o]], w=[bcb[BC_Z1]])
        E("vector", lambda e: e.tensor_tensor(
            BCV[:, BC_HMAX:BC_HMAX + 1], BCV[:, BC_HMAX:BC_HMAX + 1],
            BCV[:, BC_Z1:BC_Z1 + 1], ALU.max),
          r=[bcb[BC_Z1]], w=[bcb[BC_HMAX]])
    emit_w2_round(HC1)

    DMA("gpsimd", lambda e: e.dma_start(
        out=cc2i_d[:], in_=BCV[:, BC_HMAX:BC_HMAX + 1]),
        r=[bcb[BC_HMAX]], w=[cc2ib])
    E("gpsimd", lambda e: e.collective_compute(
        "AllGather", ALU.bypass, replica_groups=RG,
        ins=[cc2i_d.ap().opt()], outs=[cc2o_d.ap().opt()]),
      r=[cc2ib], w=[cc2ob], sem=("cc", 1), extra=[("cc", 1)])
    HT, htb = AR.carve_n(64 * KB, 2, 16 * KB, F32, ncols=4096)
    HQ, hqb = AR.carve_n(96 * KB, 2, 8 * KB, BF16, ncols=4096)
    X2TOK_l, x2tb_l = AR.carve_n(112 * KB, 1, 4 * KB, F32, ncols=1024)
    X2TOK, x2tb = X2TOK_l[0], x2tb_l[0]
    outdb = Buf()

    X2TOK2_l, x2tb2_l = AR.carve_n(116 * KB, 1, 4 * KB, F32, ncols=1024)
    X2TOKS = [X2TOK, X2TOK2_l[0]]
    x2tbs = [x2tb, x2tb2_l[0]]

    def p4_ht_load(tc):
        i = tc % 2
        DMA("sync", lambda e, tc=tc, i=i: e.dma_start(
            out=HT[i].rearrange("p (kc t) -> p kc t", t=128),
            in_=hsp_d[tc].rearrange("kc p t -> p kc t")),
            r=[hspb[tc]], w=[htb[i]])

    hqcb = [[Buf() for _ in range(4)] for _ in range(2)]

    def p4_hq_round(tc):
        i = tc % 2
        for c in range(4):
            W4 = min(1024, (T * 2) // 1)
            E("scalar", lambda e, i=i, c=c: e.activation(
                HT[i][:, c * 1024:(c + 1) * 1024],
                HT[i][:, c * 1024:(c + 1) * 1024], AF.Copy, bias=MAGIC,
                scale=BCV[:, BC_INVH:BC_INVH + 1]),
              r=[htb[i], bcb[BC_INVH]], w=[htb[i]])
            E("vector", lambda e, i=i, c=c: e.tensor_scalar(
                HQ[i][:, c * 1024:(c + 1) * 1024],
                HT[i][:, c * 1024:(c + 1) * 1024], MAGIC, None, ALU.subtract),
              r=[htb[i]], w=[hqb[i], hqcb[i][c]])

    def p4_load_round(tc):
        p4_ht_load(tc)
        p4_hq_round(tc)

    p4_ht_load(0)
    if NTC > 1:
        p4_ht_load(1)

    def p4_transposes(tc):
        j = tc % 2
        slc = tc * 128 // S
        for dc in range(KC):
            pj = dc % 2
            sl4 = (dc % 4) // 2
            E("tensor", lambda e, dc=dc, tc=tc, pj=pj, sl4=sl4: e.transpose(
                PM[pj][:, 128 * sl4:128 * sl4 + 128],
                X2[dc][:, tc * 128:(tc + 1) * 128], IDF[:]),
              r=[X2B[dc][slc], cb["idf"]], w=[pmb[pj]])
            E("vector", lambda e, dc=dc, pj=pj, sl4=sl4, j=j: e.tensor_copy(
                X2TOKS[j][:, dc * 128:(dc + 1) * 128],
                PM[pj][:, 128 * sl4:128 * sl4 + 128]),
              r=[pmb[pj]], w=[x2tbs[j]])

    p4_transposes(0)
    if NTC > 1:
        p4_transposes(1)

    land_gather(cc2o_d, SC_AMH)
    scale_chain(SC_AMH, SC_TH, SC_SH, SC_INVH)
    bcast_scal(SC_INVH, scb[SC_INVH], BC_INVH)
    E("vector", lambda e: e.tensor_tensor(
        SCAL[0:1, SC_S2X:SC_S2X + 1], SCAL[0:1, SC_SW2:SC_SW2 + 1],
        SCAL[0:1, SC_SH:SC_SH + 1], ALU.mult),
      r=[scb[SC_SW2], scb[SC_SH]], w=[scb[SC_S2X]])
    bcast_scal(SC_S2X, scb[SC_S2X], BC_S2X)


    def p4_mm2(tc):
        i = tc % 2
        for c in range(4):
            def mm2_chunk(e, i=i, c=c):
                last = None
                for k in range(c * 8, (c + 1) * 8):
                    for half in range(2):
                        last = e.matmul(
                            PBW[i][:, half * 512:(half + 1) * 512],
                            HQ[i][:, k * 128:(k + 1) * 128],
                            W2Q[k][:, half * 512:(half + 1) * 512],
                            start=(k == 0), stop=False)
                if c == 3:
                    for half in range(2):
                        last = e.matmul(
                            PBW[i][:, half * 512:(half + 1) * 512], ONESB[:],
                            B2R[0:1, half * 512:(half + 1) * 512],
                            start=False, stop=True)
                return last
            E("tensor", mm2_chunk,
              r=[hqcb[i][c], cb["onesb"], cb["b2r"]]
                + [w2qb[k] for k in range(c * 8, (c + 1) * 8)],
              w=[pwb[i][0], pwb[i][1]])

    def p4_finish(tc):
        i = tc % 2
        j = tc % 2
        E("vector", lambda e, i=i, j=j: e.scalar_tensor_tensor(
            POOLT[i][:], PBW[i][:, 0:1024], BCV[:, BC_S2X:BC_S2X + 1],
            X2TOKS[j][:], ALU.mult, ALU.add),
          r=[pwb[i][0], pwb[i][1], bcb[BC_S2X], x2tbs[j]], w=[ptb[i]])
        DMA("gpsimd", lambda e, tc=tc, i=i: e.dma_start(
            out=out_d[tc * 128:(tc + 1) * 128, :], in_=POOLT[i][:]),
            r=[ptb[i]], w=[outdb])

    p4_hq_round(0)
    if NTC > 1:
        p4_hq_round(1)
    for tc in range(NTC):
        p4_mm2(tc)
        if tc + 2 < NTC:
            p4_ht_load(tc + 2)
            p4_hq_round(tc + 2)
        p4_finish(tc)
        if tc + 2 < NTC:
            p4_transposes(tc + 2)

    pr.finish()
    return nc


def _prep(inputs, T=2048):
    f32 = np.float32
    bf = ml_dtypes.bfloat16
    x = np.asarray(inputs["L0_attn_x"], f32).reshape(-1, D)
    w_qkv = np.asarray(inputs["L0_attn_w_qkv"], f32)
    w_out = np.asarray(inputs["L0_attn_w_out"], f32)
    b_out = np.asarray(inputs["L0_attn_b_out"], f32)
    proj = np.asarray(inputs["L0_attn_proj"], f32)
    w1 = np.asarray(inputs["L0_ffn_w1"], f32)
    b1 = np.asarray(inputs["L0_ffn_b1"], f32)
    w2 = np.asarray(inputs["L0_ffn_w2"], f32)
    b2 = np.asarray(inputs["L0_ffn_b2"], f32)

    shared = {
        "wqkvT": np.ascontiguousarray(w_qkv.T),
        "woutT": np.ascontiguousarray(w_out.T),
        "projE": np.ascontiguousarray(proj.transpose(1, 0, 2).reshape(128, NH * 128)),
        "boutpp": np.ascontiguousarray(b_out.reshape(8, 128).T),
        "b1pp": np.ascontiguousarray(b1.reshape(32, 128).T),
        "b2row": np.ascontiguousarray(b2.reshape(1, D)),
        "w1blk": np.ascontiguousarray(
            w1.T.reshape(KC, 128, HC1, 128).transpose(2, 0, 1, 3)),
        "w2T": np.ascontiguousarray(w2.T),
        "idf": np.eye(128, dtype=f32),
        "idb": np.eye(128, dtype=f32).astype(bf),
        "onesr": np.ones((1, 128), f32),
        "onesb": np.ones((1, 128), f32).astype(bf),
        "invd": np.full((128, 1), 1.0 / 1024.0, f32).astype(bf),
        "cvec": np.broadcast_to(
            np.array([12582912.0, 1e-5, -0.5, 0.0], f32), (128, 4)).copy(),
    }
    n_sh = x.shape[0] // T
    in_maps = []
    for c in range(n_sh):
        m = dict(shared)
        m["xT"] = np.ascontiguousarray(x[c * T:(c + 1) * T, :].T)
        in_maps.append(m)
    return in_maps


def kernel(**inputs):
    T = 2048
    nc = build(T)
    in_maps = _prep(inputs, T)
    res = run_bass_kernel_spmd(nc, in_maps, core_ids=list(range(NCORES)))
    shards = [np.asarray(res.results[c]["out"]) for c in range(NCORES)]
    full = np.concatenate(shards, axis=0).reshape(4, 4096, D)
    return full.astype(np.float32)
